# version 11
# baseline (speedup 1.0000x reference)
import numpy as np
import ml_dtypes
import concourse.bass as bass
import concourse.mybir as mybir
from concourse.bass_utils import run_bass_kernel_spmd
from contextlib import ExitStack

F32 = mybir.dt.float32
BF16 = mybir.dt.bfloat16
AF = mybir.ActivationFunctionType
ALU = mybir.AluOpType
AX = mybir.AxisListType

D = 1024
BIG = 1e30

TEN, VEC, SCA, GPS, SYN = "tsem", "vsem", "ssem", "gsem", "ysem"
DG, DY = "dg", "dy"
ENGINES = [TEN, VEC, SCA, GPS, SYN]


class Res:
    __slots__ = ("w", "rs")

    def __init__(self):
        self.w = None
        self.rs = []


NSLOT = 10


class Prog:
    def __init__(self, nc, st):
        self.nc = nc
        self.sem = {}
        self.cnt = {}
        for name in ENGINES:
            self.sem[name] = st.enter_context(nc.semaphore(name))
            self.cnt[name] = 0
        self.dslot = {}
        for q in (DG, DY):
            for i in range(NSLOT):
                nm = f"{q}{i}"
                self.sem[nm] = st.enter_context(nc.semaphore(nm))
                self.cnt[nm] = 0
            self.dslot[q] = 0
        self.eng = {TEN: nc.tensor, VEC: nc.vector, SCA: nc.scalar,
                    GPS: nc.gpsimd, SYN: nc.sync}
        self.dma_eng = {DG: nc.gpsimd, DY: nc.sync}
        self.dma_issuer = {DG: GPS, DY: SYN}
        self.waited = {k: {} for k in ENGINES}

    def _wait(self, issuer_key, eng_obj, sem_name, val):
        if val <= 0:
            return
        w = self.waited[issuer_key]
        if w.get(sem_name, 0) >= val:
            return
        eng_obj.wait_ge(self.sem[sem_name], val)
        w[sem_name] = val

    def _dep_tok(self, issuer_key, eng_obj, tok):
        if tok is not None:
            self._wait(issuer_key, eng_obj, tok[0], tok[1])

    def _deps(self, issuer_key, eng_obj, reads, writes):
        for res in reads:
            self._dep_tok(issuer_key, eng_obj, res.w)
        for res in writes:
            if res.w is not None and res.w[0] != issuer_key:
                self._dep_tok(issuer_key, eng_obj, res.w)
            for tok in res.rs:
                if tok[0] != issuer_key:
                    self._dep_tok(issuer_key, eng_obj, tok)

    def op(self, ekey, fn, reads=(), writes=(), deps=()):
        eng = self.eng[ekey]
        for tok in deps:
            self._dep_tok(ekey, eng, tok)
        self._deps(ekey, eng, reads, writes)
        inst = fn(eng)
        self.cnt[ekey] += 1
        val = self.cnt[ekey]
        inst.then_inc(self.sem[ekey], 1)
        tok = (ekey, val)
        for res in writes:
            res.w = tok
            res.rs = []
        for res in reads:
            res.rs.append(tok)
        return tok

    def dma(self, qkey, out_ap, in_ap, reads=(), writes=(), deps=()):
        eng = self.dma_eng[qkey]
        ikey = self.dma_issuer[qkey]
        slot = self.dslot[qkey]
        self.dslot[qkey] = (slot + 1) % NSLOT
        snm = f"{qkey}{slot}"
        self._wait(ikey, eng, snm, self.cnt[snm])
        for tok in deps:
            self._dep_tok(ikey, eng, tok)
        self._deps(ikey, eng, reads, writes)
        inst = eng.dma_start(out_ap, in_ap)
        self.cnt[snm] += 16
        val = self.cnt[snm]
        inst.then_inc(self.sem[snm], 16)
        tok = (snm, val)
        for res in writes:
            res.w = tok
            res.rs = []
        for res in reads:
            res.rs.append(tok)
        return tok

    def fence(self):
        return [(k, v) for k, v in self.cnt.items() if v > 0]

    def barrier(self, deps):
        for ekey in ENGINES:
            eng = self.eng[ekey]
            for tok in deps:
                self._dep_tok(ekey, eng, tok)


def bview(t, offset, dims):
    return bass.AP(t, offset, dims)



def emit_ffn(p, st, fence, lay, xin, xout, TC):
    nc = p.nc
    name = lay["name"]
    E = lay["E"]
    G = lay["n_group"]
    EG = E // G
    K = lay["top_k"]
    TH = TC // 2
    NCH = TH // 512
    NTB = TH // 128
    NE = NTB * E
    NG = NTB * G

    p.barrier(fence)

    sb = st.enter_context
    acc = sb(nc.sbuf_tensor(f"{name}_acc", [128, NTB * D], F32))
    h_b = sb(nc.sbuf_tensor(f"{name}_h", [128, 8 * TH], BF16))
    lnb = sb(nc.sbuf_tensor(f"{name}_lnb", [128, 4 * D], BF16))
    stat = sb(nc.sbuf_tensor(f"{name}_stat", [128, 48], F32))
    sgate = sb(nc.sbuf_tensor(f"{name}_sg", [128, NE], F32))
    scb = sb(nc.sbuf_tensor(f"{name}_scb", [128, NE], F32))
    mwork = sb(nc.sbuf_tensor(f"{name}_mw", [128, NE], F32))
    gtmp = sb(nc.sbuf_tensor(f"{name}_gt", [128, NE], F32))
    e1b = sb(nc.sbuf_tensor(f"{name}_e1", [128, NE], F32))
    mjs = sb(nc.sbuf_tensor(f"{name}_mjs", [128, NTB * 20], F32))
    cwb = sb(nc.sbuf_tensor(f"{name}_cw", [128, NE], F32))
    acwb = sb(nc.sbuf_tensor(f"{name}_acw", [128, NTB * 16], F32))
    gwT = sb(nc.sbuf_tensor(f"{name}_gwT", [128, 8 * E], BF16))
    gb_b = sb(nc.sbuf_tensor(f"{name}_gbb", [128, E], F32))
    gnat = sb(nc.sbuf_tensor(f"{name}_gnat", [128, 1024], BF16))
    wg_b = sb(nc.sbuf_tensor(f"{name}_wg", [128, 2 * 4096], BF16))
    wu_b = sb(nc.sbuf_tensor(f"{name}_wu", [128, 2 * 4096], BF16))
    wd_b = sb(nc.sbuf_tensor(f"{name}_wd", [128, 2 * 4096], BF16))
    silu_s = sb(nc.sbuf_tensor(f"{name}_sil", [128, 2048], BF16))
    inter_s = sb(nc.sbuf_tensor(f"{name}_int", [128, 2 * 2048], BF16))
    tmp_s = sb(nc.sbuf_tensor(f"{name}_tmp", [128, 2 * D], F32))
    sqd = tmp_s

    units = lay["units"]
    NU = len(units)

    r_acc = [Res() for _ in range(NTB)]
    r_hcol = [Res() for _ in range(NTB)]
    r_lnb = [Res() for _ in range(4)]
    r_stat = Res()
    r_sq = [Res() for _ in range(4)]
    r_sqd = Res()
    _rs = Res()
    r_sgate = [_rs, _rs]
    r_gate = Res()
    _rc = Res()
    r_cw = [_rc, _rc]
    r_gwT = Res()
    r_gnat = Res()
    r_gbb = Res()
    r_wg = [Res() for _ in range(2)]
    r_wu = [Res() for _ in range(2)]
    r_wd = [Res() for _ in range(2)]
    r_silu = [Res() for _ in range(4)]
    r_inter = [[Res() for _ in range(4)] for _ in range(2)]
    r_tmp = [Res() for _ in range(2)]

    ident = lay["ident"]

    def load_unit(u):
        wg_ap, wu_ap, wd_ap, I_u, _, _ = units[u]
        b = u % 2
        p.dma(DG, wg_b[:, b * 4096: b * 4096 + 8 * I_u].rearrange("p (k i) -> p k i", k=8),
              wg_ap.rearrange("(k q) i -> q k i", q=128), writes=[r_wg[b]])
        p.dma(DG, wu_b[:, b * 4096: b * 4096 + 8 * I_u].rearrange("p (k i) -> p k i", k=8),
              wu_ap.rearrange("(k q) i -> q k i", q=128), writes=[r_wu[b]])
        p.dma(DG, wd_b[:, b * 4096: b * 4096 + (I_u // 128) * D].rearrange("p (k j) -> p k j", j=D),
              wd_ap.rearrange("(k q) j -> q k j", q=128), writes=[r_wd[b]])

    kind, gw_ap = lay["gwT_src"]
    if kind == "direct":
        p.dma(DG, gwT[:, :].rearrange("p (k e) -> p k e", k=8),
              gw_ap.rearrange("(k q) e -> q k e", q=128), writes=[r_gwT])
    else:
        p.dma(DG, gnat[0:E, 0:1024], gw_ap, writes=[r_gnat])
    p.dma(DG, gb_b[:, :], lay["gb_bcast"], writes=[r_gbb])

    for hh in range(2):
        tok0 = hh * TH

        with ExitStack() as pa:
            ptr = [pa.enter_context(nc.psum_tensor(f"{name}_pt{hh}{i}", [128, 128], BF16))
                   for i in range(2)]
            plog = pa.enter_context(nc.psum_tensor(f"{name}_pl{hh}", [128, 512], F32))
            r_ptr = [Res(), Res()]
            r_plog = Res()
            pfence = p.fence()

            if hh == 0 and kind == "transpose":
                for k in range(8):
                    p.op(TEN, lambda e, k=k: e.transpose(
                        ptr[k % 2][0:128, 0:E], gnat[0:E, k * 128:(k + 1) * 128],
                        ident[0:E, 0:E]),
                        reads=[r_gnat], writes=[r_ptr[k % 2]], deps=pfence)
                    p.op(SCA, lambda e, k=k: e.copy(gwT[:, k * E:(k + 1) * E],
                                                    ptr[k % 2][0:128, 0:E]),
                         reads=[r_ptr[k % 2]], writes=[r_gwT])

            if hh == 0:
                load_unit(0)
                if NU > 1:
                    load_unit(1)

            for c in range(NCH):
                p.dma(DY,
                      acc[:, :].rearrange("p (tb d) -> p tb d", d=D)[:, c * 4:(c + 1) * 4, :],
                      xin[tok0 + c * 512: tok0 + (c + 1) * 512, :].rearrange(
                          "(tb q) d -> q tb d", q=128),
                      writes=r_acc[c * 4:c * 4 + 4])
                p.op(VEC, lambda e, c=c: e.tensor_reduce(
                    stat[:, 0:4].rearrange("p (a o) -> p a o", o=1),
                    acc[:, :].rearrange("p (tb d) -> p tb d", d=D)[:, c * 4:(c + 1) * 4, :],
                    AX.X, ALU.add),
                    reads=r_acc[c * 4:c * 4 + 4], writes=[r_stat])
                for tb in range(4):
                    p.op(SCA, lambda e, c=c, tb=tb: e.activation(
                        sqd[:, 0:D], acc[:, (c * 4 + tb) * D:(c * 4 + tb + 1) * D],
                        AF.Square, accum_out=stat[:, 8 + tb: 9 + tb]),
                        reads=[r_acc[c * 4 + tb], r_sqd], writes=[r_sq[tb], r_sqd, r_tmp[0]])
                p.op(VEC, lambda e: e.tensor_scalar(stat[:, 0:4], stat[:, 0:4],
                                                    1.0 / D, None, ALU.mult),
                     reads=[r_stat], writes=[r_stat])
                p.op(VEC, lambda e: e.tensor_scalar(stat[:, 12:16], stat[:, 8:12],
                                                    1.0 / D, None, ALU.mult),
                     reads=[r_sq[0], r_sq[1], r_sq[2], r_sq[3]], writes=[r_stat])
                p.op(VEC, lambda e: e.tensor_tensor(stat[:, 20:24], stat[:, 0:4],
                                                    stat[:, 0:4], ALU.mult),
                     reads=[r_stat], writes=[r_stat])
                p.op(VEC, lambda e: e.tensor_tensor(stat[:, 12:16], stat[:, 12:16],
                                                    stat[:, 20:24], ALU.subtract),
                     reads=[r_stat], writes=[r_stat])
                p.op(VEC, lambda e: e.tensor_scalar(stat[:, 12:16], stat[:, 12:16],
                                                    1e-5, None, ALU.add),
                     reads=[r_stat], writes=[r_stat])
                p.op(SCA, lambda e: e.activation(stat[:, 24:28], stat[:, 12:16], AF.Sqrt),
                     reads=[r_stat], writes=[r_stat])
                p.op(VEC, lambda e: e.reciprocal(stat[:, 28:32], stat[:, 24:28]),
                     reads=[r_stat], writes=[r_stat])
                for tb in range(4):
                    p.op(VEC, lambda e, c=c, tb=tb: e.tensor_scalar(
                        lnb[:, tb * D:(tb + 1) * D],
                        acc[:, (c * 4 + tb) * D:(c * 4 + tb + 1) * D],
                        stat[:, tb: tb + 1], stat[:, 28 + tb: 29 + tb],
                        ALU.subtract, op1=ALU.mult),
                        reads=[r_acc[c * 4 + tb], r_stat], writes=[r_lnb[tb]])
                    for k in range(8):
                        p.op(TEN, lambda e, tb=tb, k=k: e.transpose(
                            ptr[k % 2][:, :], lnb[:, tb * D + k * 128: tb * D + (k + 1) * 128],
                            ident[:, :]),
                            reads=[r_lnb[tb]], writes=[r_ptr[k % 2]], deps=pfence)
                        p.op(SCA, lambda e, c=c, tb=tb, k=k: e.copy(
                            h_b[:, k * TH + c * 512 + tb * 128: k * TH + c * 512 + (tb + 1) * 128],
                            ptr[k % 2][:, :]),
                            reads=[r_ptr[k % 2]], writes=[r_hcol[c * 4 + tb]])
                for tb in range(4):
                    def mk(e, c=c, tb=tb):
                        mm = None
                        for k in range(8):
                            mm = e.matmul(
                                plog[:, (c * 4 + tb) * E:(c * 4 + tb + 1) * E],
                                h_b[:, k * TH + c * 512 + tb * 128: k * TH + c * 512 + (tb + 1) * 128],
                                gwT[:, k * E:(k + 1) * E],
                                start=(k == 0), stop=(k == 7))
                        return mm
                    p.op(TEN, mk, reads=[r_hcol[c * 4 + tb], r_gwT], writes=[r_plog],
                         deps=pfence)

            p.op(SCA, lambda e: e.activation(sgate[:, 0:NE],
                                             plog[:, 0:NE], AF.Sigmoid),
                 reads=[r_plog], writes=[r_sgate[hh]])

            sg_ap = sgate[:, 0:NE]
            slot_thr = 2 * NG
            slot_top = 2 * NG + NTB
            slot_den = 2 * NG + 7 * NTB
            slot_inv = 2 * NG + 8 * NTB

            def g_op(fn, extra_reads=(), wr=None):
                p.op(VEC, fn, reads=[r_gate] + list(extra_reads),
                     writes=[wr if wr is not None else r_gate])

            g_op(lambda e: e.tensor_tensor(
                scb[:, :].rearrange("p (t q) -> p t q", q=E),
                sg_ap.rearrange("p (t q) -> p t q", q=E),
                bview(gb_b, 0, [[E, 128], [0, NTB], [1, E]]), ALU.add),
                extra_reads=[r_sgate[hh], r_gbb])
            sc4 = scb[:, :].rearrange("p (tg q) -> p tg q", q=EG)
            g_op(lambda e: e.tensor_reduce(
                mjs[:, 0:NG].rearrange("p (a o) -> p a o", o=1), sc4, AX.X, ALU.max))
            g_op(lambda e: e.tensor_tensor(
                gtmp[:, :].rearrange("p (tg q) -> p tg q", q=EG), sc4,
                bview(mjs, 0, [[NTB * 20, 128], [1, NG], [0, EG]]), ALU.is_ge))
            g_op(lambda e: e.tensor_scalar(gtmp[:, :], gtmp[:, :], BIG, None, ALU.mult))
            g_op(lambda e: e.tensor_tensor(mwork[:, :], scb[:, :], gtmp[:, :], ALU.subtract))
            g_op(lambda e: e.tensor_reduce(
                mjs[:, NG:2 * NG].rearrange("p (a o) -> p a o", o=1),
                mwork[:, :].rearrange("p (tg q) -> p tg q", q=EG), AX.X, ALU.max))
            g_op(lambda e: e.tensor_tensor(mjs[:, 0:NG], mjs[:, 0:NG],
                                           mjs[:, NG:2 * NG], ALU.add))
            gs3 = mjs[:, 0:NG].rearrange("p (t g) -> p t g", g=G)
            g_op(lambda e: e.tensor_reduce(
                mjs[:, slot_thr:slot_thr + NTB].rearrange("p (a o) -> p a o", o=1),
                gs3, AX.X, ALU.max))
            g_op(lambda e: e.tensor_tensor(
                gtmp[:, 0:NG].rearrange("p (t g) -> p t g", g=G), gs3,
                bview(mjs, slot_thr, [[NTB * 20, 128], [1, NTB], [0, G]]), ALU.is_ge))
            g_op(lambda e: e.tensor_scalar(gtmp[:, 0:NG], gtmp[:, 0:NG], BIG, None, ALU.mult))
            g_op(lambda e: e.tensor_tensor(gtmp[:, NG:2 * NG], mjs[:, 0:NG],
                                           gtmp[:, 0:NG], ALU.subtract))
            g_op(lambda e: e.tensor_reduce(
                mjs[:, slot_thr:slot_thr + NTB].rearrange("p (a o) -> p a o", o=1),
                gtmp[:, NG:2 * NG].rearrange("p (t g) -> p t g", g=G), AX.X, ALU.max))
            g_op(lambda e: e.tensor_tensor(
                gtmp[:, 0:NG].rearrange("p (t g) -> p t g", g=G), gs3,
                bview(mjs, slot_thr, [[NTB * 20, 128], [1, NTB], [0, G]]), ALU.is_ge))
            g_op(lambda e: e.tensor_scalar(e1b[:, :], scb[:, :], 0.0, None, ALU.mult))
            g_op(lambda e: e.tensor_tensor(
                e1b[:, :].rearrange("p (tg q) -> p tg q", q=EG),
                e1b[:, :].rearrange("p (tg q) -> p tg q", q=EG),
                bview(gtmp, 0, [[NE, 128], [1, NG], [0, EG]]), ALU.add))
            if lay["gate"] == "l0":
                g_op(lambda e: e.tensor_scalar(gtmp[:, :], e1b[:, :], BIG, BIG,
                                               ALU.mult, op1=ALU.subtract))
                g_op(lambda e: e.tensor_tensor(mwork[:, :], scb[:, :], e1b[:, :], ALU.mult))
                g_op(lambda e: e.tensor_tensor(mwork[:, :], mwork[:, :], gtmp[:, :], ALU.add))
            else:
                g_op(lambda e: e.tensor_tensor(mwork[:, :], scb[:, :], e1b[:, :], ALU.mult))
            g_op(lambda e: e.tensor_copy(gtmp[:, :], mwork[:, :]))
            m3 = mwork[:, :].rearrange("p (t q) -> p t q", q=E)
            for j in range(K):
                g_op(lambda e, j=j: e.tensor_reduce(
                    mjs[:, slot_top + j * NTB: slot_top + (j + 1) * NTB].rearrange(
                        "p (a o) -> p a o", o=1), m3, AX.X, ALU.max))
                if j == 0 and lay["gate"] == "l0":
                    g_op(lambda e: e.tensor_tensor(
                        e1b[:, :].rearrange("p (t q) -> p t q", q=E), m3,
                        bview(mjs, slot_top, [[NTB * 20, 128], [1, NTB], [0, E]]),
                        ALU.is_ge))
                if j < K - 1:
                    g_op(lambda e, j=j: e.tensor_tensor(
                        scb[:, :].rearrange("p (t q) -> p t q", q=E), m3,
                        bview(mjs, slot_top + j * NTB,
                              [[NTB * 20, 128], [1, NTB], [0, E]]), ALU.is_ge))
                    g_op(lambda e: e.tensor_scalar(scb[:, :], scb[:, :], BIG, None, ALU.mult))
                    g_op(lambda e: e.tensor_tensor(mwork[:, :], mwork[:, :],
                                                   scb[:, :], ALU.subtract))
            g_op(lambda e: e.tensor_tensor(
                mwork[:, :].rearrange("p (t q) -> p t q", q=E),
                gtmp[:, :].rearrange("p (t q) -> p t q", q=E),
                bview(mjs, slot_top + (K - 1) * NTB,
                      [[NTB * 20, 128], [1, NTB], [0, E]]), ALU.is_ge))
            wsrc = sg_ap if lay["gate"] == "l0" else gtmp[:, :]
            g_op(lambda e: e.tensor_tensor(cwb[:, 0:NE],
                                           mwork[:, :], wsrc, ALU.mult),
                 extra_reads=[r_sgate[hh]])
            g_op(lambda e: e.tensor_reduce(
                mjs[:, slot_den:slot_den + NTB].rearrange("p (a o) -> p a o", o=1),
                cwb[:, 0:NE].rearrange("p (t q) -> p t q", q=E),
                AX.X, ALU.add))
            g_op(lambda e: e.tensor_scalar(mjs[:, slot_den:slot_den + NTB],
                                           mjs[:, slot_den:slot_den + NTB],
                                           1e-20, None, ALU.add))
            g_op(lambda e: e.reciprocal(mjs[:, slot_inv:slot_inv + NTB],
                                        mjs[:, slot_den:slot_den + NTB]))
            g_op(lambda e: e.tensor_scalar(mjs[:, slot_inv:slot_inv + NTB],
                                           mjs[:, slot_inv:slot_inv + NTB],
                                           lay["route_scale"] * 0.5, None, ALU.mult))
            g_op(lambda e: e.tensor_tensor(
                cwb[:, 0:NE].rearrange("p (t q) -> p t q", q=E),
                cwb[:, 0:NE].rearrange("p (t q) -> p t q", q=E),
                bview(mjs, slot_inv, [[NTB * 20, 128], [1, NTB], [0, E]]), ALU.mult),
                wr=r_cw[hh])
            if lay["gate"] == "l0":
                e1p = e1b[:, :].rearrange("p (tl two) -> p tl two", two=2)
                g_op(lambda e: e.tensor_tensor(
                    acwb[:, 0:NTB * 16].rearrange(
                        "p (a o) -> p a o", o=1),
                    e1p[:, :, 0:1], e1p[:, :, 1:2], ALU.add))
                p.op(VEC, lambda e: e.tensor_scalar(
                    acwb[:, 0:NTB * 16],
                    acwb[:, 0:NTB * 16],
                    0.1 * 0.5, None, ALU.mult),
                    reads=[r_gate], writes=[r_cw[hh]])

            fence_a = [("ssem", p.cnt["ssem"])]

        with ExitStack() as pb:
            pg = [pb.enter_context(nc.psum_tensor(f"{name}_pg{hh}{i}", [128, 512], F32))
                  for i in range(4)]
            py = [pb.enter_context(nc.psum_tensor(f"{name}_py{hh}{i}", [128, 1024], F32))
                  for i in range(2)]
            r_pg = [Res() for _ in range(4)]
            r_py = [Res() for _ in range(2)]
            first_pg = [True] * 4
            first_py = [True] * 2

            for u in range(NU):
                wg_ap, wu_ap, wd_ap, I_u, cw_kind, idx = units[u]
                b = u % 2
                MU = I_u // 128
                for c in range(NCH):
                    for m in range(MU):
                        def mkg(e, b=b, m=m, c=c, I_u=I_u):
                            mm = None
                            for k in range(8):
                                mm = e.matmul(
                                    pg[m][:, 0:512],
                                    wg_b[:, b * 4096 + k * I_u + m * 128:
                                         b * 4096 + k * I_u + (m + 1) * 128],
                                    h_b[:, k * TH + c * 512: k * TH + (c + 1) * 512],
                                    start=(k == 0), stop=(k == 7))
                            return mm
                        extra = fence_a if first_pg[m] else []
                        first_pg[m] = False
                        p.op(TEN, mkg,
                             reads=[r_wg[b]] + r_hcol[c * 4: c * 4 + 4],
                             writes=[r_pg[m]], deps=extra)
                    for m in range(MU):
                        p.op(SCA, lambda e, m=m: e.activation(
                            silu_s[:, m * 512:(m + 1) * 512], pg[m][:, 0:512], AF.Silu),
                            reads=[r_pg[m]], writes=[r_silu[m]])
                    for m in range(MU):
                        def mku(e, b=b, m=m, c=c, I_u=I_u):
                            mm = None
                            for k in range(8):
                                mm = e.matmul(
                                    pg[m][:, 0:512],
                                    wu_b[:, b * 4096 + k * I_u + m * 128:
                                         b * 4096 + k * I_u + (m + 1) * 128],
                                    h_b[:, k * TH + c * 512: k * TH + (c + 1) * 512],
                                    start=(k == 0), stop=(k == 7))
                            return mm
                        p.op(TEN, mku,
                             reads=[r_wu[b]] + r_hcol[c * 4: c * 4 + 4],
                             writes=[r_pg[m]])
                    cp = c % 2
                    for m in range(MU):
                        p.op(VEC, lambda e, m=m, cp=cp: e.tensor_tensor(
                            inter_s[:, cp * 2048 + m * 512: cp * 2048 + (m + 1) * 512],
                            silu_s[:, m * 512:(m + 1) * 512], pg[m][:, 0:512], ALU.mult),
                            reads=[r_silu[m], r_pg[m]], writes=[r_inter[cp][m]])
                    for tb in range(4):
                        extra = fence_a if first_py[tb % 2] else []
                        first_py[tb % 2] = False
                        for k in range(MU):
                            def mky(e, b=b, tb=tb, k=k, MU=MU, cp=cp):
                                mm = None
                                for n in range(2):
                                    mm = e.matmul(
                                        py[tb % 2][:, n * 512:(n + 1) * 512],
                                        inter_s[:, cp * 2048 + k * 512 + tb * 128:
                                                cp * 2048 + k * 512 + tb * 128 + 128],
                                        wd_b[:, b * 4096 + k * D + n * 512:
                                             b * 4096 + k * D + (n + 1) * 512],
                                        start=(k == 0), stop=(k == MU - 1))
                                return mm
                            p.op(TEN, mky, reads=[r_inter[cp][k], r_wd[b]],
                                 writes=[r_py[tb % 2]], deps=extra)
                        if cw_kind == "const":
                            p.op(SCA, lambda e, tb=tb: e.activation(
                                tmp_s[:, (tb % 2) * D:(tb % 2 + 1) * D],
                                py[tb % 2][:, 0:1024], AF.Copy, scale=0.5),
                                reads=[r_py[tb % 2]], writes=[r_tmp[tb % 2]])
                        elif cw_kind == "cw":
                            p.op(SCA, lambda e, tb=tb, c=c, idx=idx, hh=hh: e.activation(
                                tmp_s[:, (tb % 2) * D:(tb % 2 + 1) * D],
                                py[tb % 2][:, 0:1024], AF.Copy,
                                scale=cwb[:, (c * 4 + tb) * E + idx:
                                          (c * 4 + tb) * E + idx + 1]),
                                reads=[r_py[tb % 2], r_cw[hh]], writes=[r_tmp[tb % 2]])
                        else:
                            p.op(SCA, lambda e, tb=tb, c=c, idx=idx, hh=hh: e.activation(
                                tmp_s[:, (tb % 2) * D:(tb % 2 + 1) * D],
                                py[tb % 2][:, 0:1024], AF.Copy,
                                scale=acwb[:, (c * 4 + tb) * 16 + idx:
                                           (c * 4 + tb) * 16 + idx + 1]),
                                reads=[r_py[tb % 2], r_cw[hh]], writes=[r_tmp[tb % 2]])
                        p.op(VEC, lambda e, c=c, tb=tb: e.tensor_tensor(
                            acc[:, (c * 4 + tb) * D:(c * 4 + tb + 1) * D],
                            acc[:, (c * 4 + tb) * D:(c * 4 + tb + 1) * D],
                            tmp_s[:, (tb % 2) * D:(tb % 2 + 1) * D], ALU.add),
                            reads=[r_tmp[tb % 2]], writes=[r_acc[c * 4 + tb]])
                nxt = u + 2
                if nxt < NU:
                    load_unit(nxt)
                elif hh == 0 and nxt - NU < 2:
                    load_unit(nxt - NU)

            for c in range(NCH):
                p.dma(DY,
                      xout[tok0 + c * 512: tok0 + (c + 1) * 512, :].rearrange(
                          "(tb q) d -> q tb d", q=128),
                      acc[:, :].rearrange("p (tb d) -> p tb d", d=D)[:, c * 4:(c + 1) * 4, :],
                      reads=r_acc[c * 4: c * 4 + 4])

    return p.fence()



def emit_ln4(p, L, reads_x):
    xa, lnb, sqd, stat = L["xa"], L["lnb"], L["sqd"], L["stat"]
    r_stat, r_sq, r_sqd, r_lnb = L["r_stat"], L["r_sq"], L["r_sqd"], L["r_lnb"]
    p.op(VEC, lambda e: e.tensor_reduce(
        stat[:, 0:4].rearrange("p (a o) -> p a o", o=1),
        xa[:, :].rearrange("p (tb d) -> p tb d", d=D), AX.X, ALU.add),
        reads=reads_x, writes=[r_stat])
    for tb in range(4):
        p.op(SCA, lambda e, tb=tb: e.activation(
            sqd[:, :], xa[:, tb * D:(tb + 1) * D],
            AF.Square, accum_out=stat[:, 8 + tb: 9 + tb]),
            reads=[reads_x[tb], r_sqd], writes=[r_sq[tb], r_sqd])
    p.op(VEC, lambda e: e.tensor_scalar(stat[:, 0:4], stat[:, 0:4],
                                        1.0 / D, None, ALU.mult),
         reads=[r_stat], writes=[r_stat])
    p.op(VEC, lambda e: e.tensor_scalar(stat[:, 12:16], stat[:, 8:12],
                                        1.0 / D, None, ALU.mult),
         reads=[r_sq[0], r_sq[1], r_sq[2], r_sq[3]], writes=[r_stat])
    p.op(VEC, lambda e: e.tensor_tensor(stat[:, 20:24], stat[:, 0:4],
                                        stat[:, 0:4], ALU.mult),
         reads=[r_stat], writes=[r_stat])
    p.op(VEC, lambda e: e.tensor_tensor(stat[:, 12:16], stat[:, 12:16],
                                        stat[:, 20:24], ALU.subtract),
         reads=[r_stat], writes=[r_stat])
    p.op(VEC, lambda e: e.tensor_scalar(stat[:, 12:16], stat[:, 12:16],
                                        1e-5, None, ALU.add),
         reads=[r_stat], writes=[r_stat])
    p.op(SCA, lambda e: e.activation(stat[:, 24:28], stat[:, 12:16], AF.Sqrt),
         reads=[r_stat], writes=[r_stat])
    p.op(VEC, lambda e: e.reciprocal(stat[:, 28:32], stat[:, 24:28]),
         reads=[r_stat], writes=[r_stat])
    for tb in range(4):
        p.op(VEC, lambda e, tb=tb: e.tensor_scalar(
            lnb[:, tb * D:(tb + 1) * D], xa[:, tb * D:(tb + 1) * D],
            stat[:, tb: tb + 1], stat[:, 28 + tb: 29 + tb],
            ALU.subtract, op1=ALU.mult),
            reads=[reads_x[tb], r_stat], writes=[r_lnb[tb]])



def emit_attn0(p, st, fence, lay, xin, xout, TC):
    nc = p.nc
    name = lay["name"]
    NCH = TC // 512
    ident = lay["ident"]

    p.barrier(fence)

    sb = st.enter_context
    wt_a = sb(nc.sbuf_tensor(f"{name}_wta", [128, 8192], BF16))
    wt_b = sb(nc.sbuf_tensor(f"{name}_wtb", [128, 8192], BF16))
    wnat = sb(nc.sbuf_tensor(f"{name}_wnat", [128, 8192], BF16))
    xa = sb(nc.sbuf_tensor(f"{name}_xa", [128, 4 * D], F32))
    lnb = sb(nc.sbuf_tensor(f"{name}_lnb", [128, 4 * D], BF16))
    sqd = sb(nc.sbuf_tensor(f"{name}_sqd", [128, D], F32))
    stat = sb(nc.sbuf_tensor(f"{name}_stat", [128, 48], F32))
    h_sb = sb(nc.sbuf_tensor(f"{name}_h", [128, 8 * 512], BF16))
    kf_tm = sb(nc.sbuf_tensor(f"{name}_kf", [128, 4 * D], BF16))
    v_tm = sb(nc.sbuf_tensor(f"{name}_vt", [128, 4 * D], BF16))
    kv_sb = sb(nc.sbuf_tensor(f"{name}_kv", [128, 512], BF16))
    ks_sb = sb(nc.sbuf_tensor(f"{name}_ks", [128, 8], F32))
    blkks = sb(nc.sbuf_tensor(f"{name}_bks", [128, 128], BF16))
    qf = sb(nc.sbuf_tensor(f"{name}_qf", [128, 8 * 512], BF16))
    rden = sb(nc.sbuf_tensor(f"{name}_rd", [16, 512], F32))
    rden_bf = sb(nc.sbuf_tensor(f"{name}_rdb", [16, 512], BF16))
    ao_sb = sb(nc.sbuf_tensor(f"{name}_ao", [128, 8 * 512], BF16))
    bo_b = sb(nc.sbuf_tensor(f"{name}_bob", [128, D], F32))
    blk1 = sb(nc.sbuf_tensor(f"{name}_blk", [128, 128], BF16))
    ekp = sb(nc.sbuf_tensor(f"{name}_ekp", [16, 1024], BF16))
    onescol = sb(nc.sbuf_tensor(f"{name}_oc", [128, 1], BF16))
    emn = sb(nc.sbuf_tensor(f"{name}_emn", [128, 2 * D], F32))
    eex = sb(nc.sbuf_tensor(f"{name}_eex", [128, 2 * D], F32))
    emx = sb(nc.sbuf_tensor(f"{name}_emx", [128, 2 * D], F32))
    otmp = sb(nc.sbuf_tensor(f"{name}_ot", [128, 2 * D], F32))

    h_scr = nc.dram_tensor(f"{name}_hscr", [D, TC], BF16)

    r_wta = Res(); r_wtb = Res(); r_wnat = Res()
    r_xa = [Res() for _ in range(4)]
    r_lnb = [Res() for _ in range(4)]
    r_stat = Res(); r_sq = [Res() for _ in range(4)]; r_sqd = Res()
    r_h = [Res() for _ in range(4)]
    r_hscr = [Res() for _ in range(NCH)]
    r_kf = [Res() for _ in range(4)]
    r_vt = [Res() for _ in range(4)]
    r_kv = Res(); r_ks = Res()
    r_kvsb = Res(); r_kssb = Res()
    r_qf = [Res() for _ in range(8)]
    r_blkks = Res()
    r_rden = Res(); r_rdenb = Res()
    r_ao = [Res() for _ in range(16)]
    r_elu = [Res(), Res()]
    r_otmp = [Res(), Res()]
    r_consts = Res()

    L = dict(xa=xa, lnb=lnb, sqd=sqd, stat=stat,
             r_stat=r_stat, r_sq=r_sq, r_sqd=r_sqd, r_lnb=r_lnb)

    p.dma(DG, bo_b[:, :], lay["bo_bcast"], writes=[r_consts])
    p.dma(DG, blk1[:, :], lay["blockones"], writes=[r_consts])
    p.dma(DG, ekp[0:16, :], lay["ekpat"], writes=[r_consts])
    p.dma(DG, onescol[:, :], lay["onescol"], writes=[r_consts])

    def load_transpose_w(w_ap, dst, r_dst, ptr, r_ptr, pf):
        p.dma(DG, wnat[:, :].rearrange("p (k i) -> p k i", k=8),
              w_ap.rearrange("(k q) i -> q k i", q=128), writes=[r_wnat])
        for o in range(8):
            for i in range(8):
                j = (o * 8 + i) % 2
                p.op(TEN, lambda e, o=o, i=i, j=j: e.transpose(
                    ptr[j][:, :], wnat[:, o * 1024 + i * 128: o * 1024 + (i + 1) * 128],
                    ident[:, :]),
                    reads=[r_wnat], writes=[r_ptr[j]], deps=pf)
                p.op(SCA, lambda e, o=o, i=i, j=j: e.copy(
                    dst[:, i * 1024 + o * 128: i * 1024 + (o + 1) * 128], ptr[j][:, :]),
                    reads=[r_ptr[j]], writes=[r_dst])

    with ExitStack() as p1:
        ptr = [p1.enter_context(nc.psum_tensor(f"{name}_pt{i}", [128, 128], BF16))
               for i in range(2)]
        pk = [p1.enter_context(nc.psum_tensor(f"{name}_pk{i}", [128, 1024], F32))
              for i in range(2)]
        pkv = p1.enter_context(nc.psum_tensor(f"{name}_pkv", [128, 512], F32))
        pks = p1.enter_context(nc.psum_tensor(f"{name}_pks", [128, 8], F32))
        r_ptr = [Res(), Res()]
        r_pk = [Res(), Res()]
        r_pkv = Res(); r_pks = Res()
        pf = p.fence()

        load_transpose_w(lay["wk"], wt_a, r_wta, ptr, r_ptr, pf)
        load_transpose_w(lay["wv"], wt_b, r_wtb, ptr, r_ptr, pf)

        kv_started = {0: False, 64: False}
        ks_started = [False]
        for c in range(NCH):
            p.dma(DY, xa[:, :].rearrange("p (tb d) -> p tb d", d=D),
                  xin[c * 512:(c + 1) * 512, :].rearrange("(tb q) d -> q tb d", q=128),
                  writes=r_xa)
            emit_ln4(p, L, r_xa)
            for tb in range(4):
                for k in range(8):
                    j = k % 2
                    p.op(TEN, lambda e, tb=tb, k=k, j=j: e.transpose(
                        ptr[j][:, :], lnb[:, tb * D + k * 128: tb * D + (k + 1) * 128],
                        ident[:, :]),
                        reads=[r_lnb[tb]], writes=[r_ptr[j]], deps=pf)
                    p.op(SCA, lambda e, tb=tb, k=k, j=j: e.copy(
                        h_sb[:, k * 512 + tb * 128: k * 512 + (tb + 1) * 128], ptr[j][:, :]),
                        reads=[r_ptr[j]], writes=[r_h[tb]])
            p.dma(DY, h_scr[:, c * 512:(c + 1) * 512].rearrange("(k q) t -> q k t", q=128),
                  h_sb[:, :].rearrange("p (k t) -> p k t", k=8),
                  reads=[r_h[0], r_h[1], r_h[2], r_h[3]], writes=[r_hscr[c]])
            kv_jobs = []
            for tb in range(4):
                for n in range(2):
                    def mkk(e, tb=tb, n=n):
                        mm = None
                        for k in range(8):
                            mm = e.matmul(
                                pk[0][:, n * 512:(n + 1) * 512],
                                h_sb[:, k * 512 + tb * 128: k * 512 + (tb + 1) * 128],
                                wt_a[:, k * 1024 + n * 512: k * 1024 + (n + 1) * 512],
                                start=(k == 0), stop=(k == 7))
                        return mm
                    p.op(TEN, mkk, reads=[r_h[tb], r_wta], writes=[r_pk[0]], deps=pf)
                pe = tb % 2
                p.op(VEC, lambda e, pe=pe: e.tensor_scalar(
                    emn[:, pe * D:(pe + 1) * D], pk[0][:, :], 0.0, None, ALU.min),
                    reads=[r_pk[0], r_elu[pe]], writes=[r_elu[pe]])
                p.op(SCA, lambda e, pe=pe: e.activation(
                    eex[:, pe * D:(pe + 1) * D], emn[:, pe * D:(pe + 1) * D], AF.Exp),
                    reads=[r_elu[pe]], writes=[r_elu[pe]])
                p.op(VEC, lambda e, pe=pe: e.tensor_scalar(
                    emx[:, pe * D:(pe + 1) * D], pk[0][:, :], 0.0, None, ALU.max),
                    reads=[r_pk[0], r_elu[pe]], writes=[r_elu[pe]])
                p.op(VEC, lambda e, tb=tb, pe=pe: e.tensor_tensor(
                    kf_tm[:, tb * D:(tb + 1) * D], eex[:, pe * D:(pe + 1) * D],
                    emx[:, pe * D:(pe + 1) * D], ALU.add),
                    reads=[r_elu[pe]], writes=[r_kf[tb]])
                for n in range(2):
                    def mkv(e, tb=tb, n=n):
                        mm = None
                        for k in range(8):
                            mm = e.matmul(
                                pk[1][:, n * 512:(n + 1) * 512],
                                h_sb[:, k * 512 + tb * 128: k * 512 + (tb + 1) * 128],
                                wt_b[:, k * 1024 + n * 512: k * 1024 + (n + 1) * 512],
                                start=(k == 0), stop=(k == 7))
                        return mm
                    p.op(TEN, mkv, reads=[r_h[tb], r_wtb], writes=[r_pk[1]], deps=pf)
                p.op(SCA, lambda e, tb=tb: e.copy(v_tm[:, tb * D:(tb + 1) * D], pk[1][:, :]),
                     reads=[r_pk[1]], writes=[r_vt[tb]])
            for tb in range(4):
                for h in range(16):
                    srow = 64 * (h % 2)
                    scol = (h // 2) * 64
                    st_flag = not kv_started[srow]
                    kv_started[srow] = True
                    last = (c == NCH - 1 and tb == 3 and h >= 14)
                    p.op(TEN, lambda e, tb=tb, h=h, srow=srow, scol=scol,
                         st_flag=st_flag, last=last: e.matmul(
                             pkv[srow:srow + 64, scol:scol + 64],
                             kf_tm[:, tb * D + h * 64: tb * D + (h + 1) * 64],
                             v_tm[:, tb * D + h * 64: tb * D + (h + 1) * 64],
                             start=st_flag, stop=last, skip_group_check=True),
                         reads=[r_kf[tb], r_vt[tb]], writes=[r_kv], deps=pf)
                for db in range(8):
                    st_flag = not ks_started[0]
                    ks_started[0] = True
                    last = (c == NCH - 1 and tb == 3 and db == 7)
                    p.op(TEN, lambda e, tb=tb, db=db, st_flag=st_flag, last=last: e.matmul(
                        pks[:, db:db + 1],
                        kf_tm[:, tb * D + db * 128: tb * D + (db + 1) * 128],
                        onescol[:, :],
                        start=st_flag, stop=last, skip_group_check=True),
                        reads=[r_kf[tb], r_consts], writes=[r_ks], deps=pf)
        p.op(VEC, lambda e: e.tensor_copy(kv_sb[:, :], pkv[:, :]),
             reads=[r_kv], writes=[r_kvsb])
        p.op(VEC, lambda e: e.tensor_copy(ks_sb[:, :], pks[:, :]),
             reads=[r_ks], writes=[r_kssb])
        for db in range(8):
            p.op(VEC, lambda e, db=db: e.tensor_scalar(
                blkks[:, db * 16:(db + 1) * 16], blk1[:, db * 16:(db + 1) * 16],
                ks_sb[:, db:db + 1], None, ALU.mult),
                reads=[r_kssb, r_consts, r_blkks], writes=[r_blkks])
        load_transpose_w(lay["wq"], wt_a, r_wta, ptr, r_ptr, pf)
        load_transpose_w(lay["wo"], wt_b, r_wtb, ptr, r_ptr, pf)

    with ExitStack() as p3:
        pool = [p3.enter_context(nc.psum_tensor(f"{name}_po{i}", [128, 1024], F32))
                for i in range(2)]
        pden = p3.enter_context(nc.psum_tensor(f"{name}_pd", [16, 512], F32))
        pdb = [p3.enter_context(nc.psum_tensor(f"{name}_pb{i}", [128, 512], F32))
               for i in range(2)]
        pao = p3.enter_context(nc.psum_tensor(f"{name}_pa", [128, 512], F32))
        r_pool = [Res() for _ in range(4)]
        r_pden = Res()
        r_pdbq = [Res(), Res(), Res()]
        pf3 = p.fence()
        ptensors = [pao, pdb[0], pdb[1]]

        def pool_region(i):
            return pool[i // 2][:, (i % 2) * 512:(i % 2) * 512 + 512]

        for c in range(NCH):
            p.dma(DY, h_sb[:, :].rearrange("p (k t) -> p k t", k=8),
                  h_scr[:, c * 512:(c + 1) * 512].rearrange("(k q) t -> q k t", q=128),
                  writes=[r_h[0], r_h[1], r_h[2], r_h[3]],
                  deps=[r_hscr[c].w])
            for db in range(8):
                reg = db % 4

                def mkq(e, db=db, reg=reg):
                    mm = None
                    for k in range(8):
                        mm = e.matmul(
                            pool_region(reg),
                            wt_a[:, k * 1024 + db * 128: k * 1024 + (db + 1) * 128],
                            h_sb[:, k * 512:(k + 1) * 512],
                            start=(k == 0), stop=(k == 7))
                    return mm
                p.op(TEN, mkq, reads=[r_h[0], r_h[1], r_h[2], r_h[3], r_wta],
                     writes=[r_pool[reg]], deps=pf3)
                pe = db % 2
                p.op(VEC, lambda e, reg=reg, pe=pe: e.tensor_scalar(
                    emn[:, pe * 512:(pe + 1) * 512], pool_region(reg), 0.0, None, ALU.min),
                    reads=[r_pool[reg], r_elu[pe]], writes=[r_elu[pe]])
                p.op(SCA, lambda e, pe=pe: e.activation(
                    eex[:, pe * 512:(pe + 1) * 512], emn[:, pe * 512:(pe + 1) * 512], AF.Exp),
                    reads=[r_elu[pe]], writes=[r_elu[pe]])
                p.op(VEC, lambda e, reg=reg, pe=pe: e.tensor_scalar(
                    emx[:, pe * 512:(pe + 1) * 512], pool_region(reg), 0.0, None, ALU.max),
                    reads=[r_pool[reg], r_elu[pe]], writes=[r_elu[pe]])
                p.op(VEC, lambda e, db=db, pe=pe: e.tensor_tensor(
                    qf[:, db * 512:(db + 1) * 512], eex[:, pe * 512:(pe + 1) * 512],
                    emx[:, pe * 512:(pe + 1) * 512], ALU.add),
                    reads=[r_elu[pe]], writes=[r_qf[db]])

            def mkden(e):
                mm = None
                for db in range(8):
                    mm = e.matmul(pden[0:16, 0:512], blkks[:, db * 16:(db + 1) * 16],
                                  qf[:, db * 512:(db + 1) * 512],
                                  start=(db == 0), stop=(db == 7))
                return mm
            p.op(TEN, mkden, reads=[r_qf[i] for i in range(8)] + [r_blkks],
                 writes=[r_pden], deps=pf3)
            p.op(SCA, lambda e: e.copy(rden[:, :], pden[0:16, 0:512]),
                 reads=[r_pden], writes=[r_rden])
            p.op(VEC, lambda e: e.tensor_scalar(rden[:, :], rden[:, :], 1e-6, None, ALU.add),
                 reads=[r_rden], writes=[r_rden])
            p.op(VEC, lambda e: e.reciprocal(rden[:, :], rden[:, :]),
                 reads=[r_rden], writes=[r_rden])
            p.op(VEC, lambda e: e.tensor_copy(rden_bf[:, :], rden[:, :]),
                 reads=[r_rden], writes=[r_rdenb])
            for db in range(8):
                j = db % 2
                p.op(TEN, lambda e, db=db, j=j: e.matmul(
                    pdb[j][:, :], ekp[0:16, db * 128:(db + 1) * 128],
                    rden_bf[0:16, :], start=True, stop=True),
                    reads=[r_rdenb], writes=[r_pdbq[1 + j]], deps=pf3)
                p.op(VEC, lambda e, db=db, j=j: e.tensor_tensor(
                    qf[:, db * 512:(db + 1) * 512], qf[:, db * 512:(db + 1) * 512],
                    pdb[j][:, :], ALU.mult),
                    reads=[r_qf[db], r_pdbq[1 + j]], writes=[r_qf[db]])
            for h in range(16):
                db = h // 2
                srow = 64 * (h % 2)
                rot = h % 3

                def mkqkv(e, h=h, db=db, srow=srow, rot=rot):
                    t = ptensors[rot]
                    return e.matmul(t[srow:srow + 64, 0:512],
                                    kv_sb[srow:srow + 64,
                                          (h // 2) * 64:(h // 2 + 1) * 64],
                                    qf[srow:srow + 64, db * 512:(db + 1) * 512],
                                    start=True, stop=True)
                p.op(TEN, mkqkv, reads=[r_kvsb, r_qf[db]], writes=[r_pdbq[rot]], deps=pf3)
                p.op(SCA, lambda e, h=h, db=db, srow=srow, rot=rot: e.copy(
                    ao_sb[srow:srow + 64, db * 512:(db + 1) * 512],
                    ptensors[rot][srow:srow + 64, 0:512]),
                    reads=[r_pdbq[rot]], writes=[r_ao[h]])
            p.dma(DY, xa[:, :].rearrange("p (tb d) -> p tb d", d=D),
                  xin[c * 512:(c + 1) * 512, :].rearrange("(tb q) d -> q tb d", q=128),
                  writes=r_xa)
            for tb in range(4):
                pt = pool[tb % 2]
                regs = [r_pool[(tb % 2) * 2], r_pool[(tb % 2) * 2 + 1]]

                def mko(e, tb=tb, pt=pt):
                    mm = None
                    for n in range(2):
                        for k in range(8):
                            mm = e.matmul(
                                pt[:, n * 512:(n + 1) * 512],
                                ao_sb[:, k * 512 + tb * 128: k * 512 + (tb + 1) * 128],
                                wt_b[:, k * 1024 + n * 512: k * 1024 + (n + 1) * 512],
                                start=(k == 0), stop=(k == 7))
                    return mm
                p.op(TEN, mko, reads=[r_ao[i] for i in range(16)] + [r_wtb],
                     writes=regs, deps=pf3)
                p.op(VEC, lambda e, tb=tb, pt=pt: e.tensor_tensor(
                    otmp[:, (tb % 2) * D:(tb % 2 + 1) * D], pt[:, 0:1024],
                    bo_b[:, :], ALU.add),
                    reads=regs + [r_consts], writes=[r_otmp[tb % 2]])
                p.op(VEC, lambda e, tb=tb: e.tensor_tensor(
                    xa[:, tb * D:(tb + 1) * D], xa[:, tb * D:(tb + 1) * D],
                    otmp[:, (tb % 2) * D:(tb % 2 + 1) * D], ALU.add),
                    reads=[r_otmp[tb % 2]], writes=[r_xa[tb]])
            p.dma(DY, xout[c * 512:(c + 1) * 512, :].rearrange("(tb q) d -> q tb d", q=128),
                  xa[:, :].rearrange("p (tb d) -> p tb d", d=D),
                  reads=r_xa)

    return p.fence()



def emit_attn1(p, st, fence, lay, xin, xout, TC):
    nc = p.nc
    name = lay["name"]
    NCH = TC // 512
    NTB_TOT = TC // 128
    ident = lay["ident"]

    p.barrier(fence)

    sb = st.enter_context
    wq_n = sb(nc.sbuf_tensor(f"{name}_wq", [128, 8192], BF16))
    xa = sb(nc.sbuf_tensor(f"{name}_xa", [128, 4 * D], F32))
    stat = sb(nc.sbuf_tensor(f"{name}_stat", [128, 48], F32))
    aff = sb(nc.sbuf_tensor(f"{name}_aff", [128, 12 * 8], F32))
    affc = sb(nc.sbuf_tensor(f"{name}_affc", [128, 6 * 8], F32))
    v_tm = sb(nc.sbuf_tensor(f"{name}_vt", [128, NTB_TOT * D], BF16))
    tq_row = sb(nc.sbuf_tensor(f"{name}_tqr", [1, 512], F32))
    tkb_row = sb(nc.sbuf_tensor(f"{name}_tkr", [1, 512], F32))
    tkcol = sb(nc.sbuf_tensor(f"{name}_tkc", [128, TC // 128], F32))
    bo_b = sb(nc.sbuf_tensor(f"{name}_bob", [128, D], F32))
    onescol = sb(nc.sbuf_tensor(f"{name}_oc", [128, 1], BF16))
    onescolf = sb(nc.sbuf_tensor(f"{name}_ocf", [128, 1], F32))
    ones1r = sb(nc.sbuf_tensor(f"{name}_o1r", [1, 128], F32))
    otmp = sb(nc.sbuf_tensor(f"{name}_ot", [128, 2 * D], F32))

    tq_scr = nc.dram_tensor(f"{name}_tqscr", [1, TC], F32)

    r_wq = Res(); r_wk = Res(); r_wv = Res()
    r_tqs = [Res() for _ in range(NCH)]
    r_xa = [Res() for _ in range(4)]
    r_lnb = [Res() for _ in range(4)]
    r_stat = Res(); r_sq = [Res() for _ in range(4)]; r_sqd = Res()
    r_h = [Res() for _ in range(4)]
    r_aff = Res(); r_affc = Res()
    r_qbf = [Res(), Res()]
    r_vtp = [Res(), Res()]
    r_vtm = [Res() for _ in range(NTB_TOT)]
    r_tq = Res(); r_tkb = Res(); r_tkc = Res()
    r_tqb = Res()
    r_at = [Res(), Res()]
    r_datc = Res()
    r_os = [Res() for _ in range(8)]
    r_otmp = [Res(), Res()]
    r_consts = Res()

    p.dma(DG, bo_b[:, :], lay["bo_bcast"], writes=[r_consts])
    p.dma(DG, onescol[:, :], lay["onescol"], writes=[r_consts])
    p.dma(DG, onescolf[:, :], lay["onescolf"], writes=[r_consts])
    p.dma(DG, ones1r[0:1, :], lay["ones1r"], writes=[r_consts])
    p.dma(DG, wq_n[:, :].rearrange("p (k i) -> p k i", k=8),
          lay["Wq"].rearrange("(k q) i -> q k i", q=128), writes=[r_wq])
    for i in range(12):
        p.dma(DG, aff[:, i * 8:(i + 1) * 8], lay["aff_cols"][i], writes=[r_aff])

    for j in range(3):
        g0 = (4 * j + 0) * 8
        b0 = (4 * j + 1) * 8
        m0 = (4 * j + 2) * 8
        v0 = (4 * j + 3) * 8
        a0 = (2 * j) * 8
        c0 = (2 * j + 1) * 8
        p.op(VEC, lambda e, v0=v0, c0=c0: e.tensor_scalar(
            affc[:, c0:c0 + 8], aff[:, v0:v0 + 8], 1e-5, None, ALU.add),
            reads=[r_aff, r_affc], writes=[r_affc])
        p.op(SCA, lambda e, c0=c0: e.activation(affc[:, c0:c0 + 8],
                                                affc[:, c0:c0 + 8], AF.Sqrt),
             reads=[r_affc], writes=[r_affc])
        p.op(VEC, lambda e, c0=c0: e.reciprocal(affc[:, c0:c0 + 8], affc[:, c0:c0 + 8]),
             reads=[r_affc], writes=[r_affc])
        p.op(VEC, lambda e, g0=g0, c0=c0, a0=a0: e.tensor_tensor(
            affc[:, a0:a0 + 8], aff[:, g0:g0 + 8], affc[:, c0:c0 + 8], ALU.mult),
            reads=[r_aff, r_affc], writes=[r_affc])
        p.op(VEC, lambda e, m0=m0, a0=a0, c0=c0: e.tensor_tensor(
            affc[:, c0:c0 + 8], aff[:, m0:m0 + 8], affc[:, a0:a0 + 8], ALU.mult),
            reads=[r_aff, r_affc], writes=[r_affc])
        p.op(VEC, lambda e, a0=a0: e.tensor_scalar(
            affc[:, a0:a0 + 8], affc[:, a0:a0 + 8], 4.0, None, ALU.mult),
            reads=[r_affc], writes=[r_affc])
        p.op(VEC, lambda e, c0=c0: e.tensor_scalar(
            affc[:, c0:c0 + 8], affc[:, c0:c0 + 8], -4.0, -4.0, ALU.mult, op1=ALU.add),
            reads=[r_affc], writes=[r_affc])
        p.op(VEC, lambda e, b0=b0: e.tensor_scalar(
            stat[:, 40:48], aff[:, b0:b0 + 8], 4.0, None, ALU.mult),
            reads=[r_aff, r_affc], writes=[r_affc])
        p.op(VEC, lambda e, c0=c0: e.tensor_tensor(
            affc[:, c0:c0 + 8], affc[:, c0:c0 + 8], stat[:, 40:48], ALU.add),
            reads=[r_affc], writes=[r_affc])

    with ExitStack() as p1:
        wk_n = p1.enter_context(nc.sbuf_tensor(f"{name}_wk", [128, 8192], BF16))
        wv_n = p1.enter_context(nc.sbuf_tensor(f"{name}_wv", [128, 8192], BF16))
        qbf = p1.enter_context(nc.sbuf_tensor(f"{name}_qb", [128, 2 * 512], F32))
        vtp = p1.enter_context(nc.sbuf_tensor(f"{name}_vtp", [128, 2 * 512], BF16))
        lnb = p1.enter_context(nc.sbuf_tensor(f"{name}_lnb", [128, 4 * D], BF16))
        sqd = p1.enter_context(nc.sbuf_tensor(f"{name}_sqd", [128, D], F32))
        h_sb = p1.enter_context(nc.sbuf_tensor(f"{name}_h", [128, 8 * 512], BF16))
        ptr = [p1.enter_context(nc.psum_tensor(f"{name}_pt{i}", [128, 128], BF16))
               for i in range(2)]
        pool3 = [p1.enter_context(nc.psum_tensor(f"{name}_pp{i}", [128, 512], F32))
                 for i in range(4)]
        pqs = p1.enter_context(nc.psum_tensor(f"{name}_pqs", [1, 512], F32))
        pks2 = p1.enter_context(nc.psum_tensor(f"{name}_pks", [1, 512], F32))
        r_ptr = [Res(), Res()]
        r_pool3 = [Res() for _ in range(4)]
        r_pqs = Res(); r_pks2 = Res()
        pf = p.fence()

        p.dma(DG, wk_n[:, :].rearrange("p (k i) -> p k i", k=8),
              lay["Wk"].rearrange("(k q) i -> q k i", q=128), writes=[r_wk])
        p.dma(DG, wv_n[:, :].rearrange("p (k i) -> p k i", k=8),
              lay["Wv"].rearrange("(k q) i -> q k i", q=128), writes=[r_wv])

        L = dict(xa=xa, lnb=lnb, sqd=sqd, stat=stat,
                 r_stat=r_stat, r_sq=r_sq, r_sqd=r_sqd, r_lnb=r_lnb)

        for c in range(NCH):
            p.dma(DY, xa[:, :].rearrange("p (tb d) -> p tb d", d=D),
                  xin[c * 512:(c + 1) * 512, :].rearrange("(tb q) d -> q tb d", q=128),
                  writes=r_xa)
            emit_ln4(p, L, r_xa)
            for tb in range(4):
                for k in range(8):
                    j = k % 2
                    p.op(TEN, lambda e, tb=tb, k=k, j=j: e.transpose(
                        ptr[j][:, :], lnb[:, tb * D + k * 128: tb * D + (k + 1) * 128],
                        ident[:, :]),
                        reads=[r_lnb[tb]], writes=[r_ptr[j]], deps=pf)
                    p.op(SCA, lambda e, tb=tb, k=k, j=j: e.copy(
                        h_sb[:, k * 512 + tb * 128: k * 512 + (tb + 1) * 128], ptr[j][:, :]),
                        reads=[r_ptr[j]], writes=[r_h[tb]])
            for (wres, wbuf, prow, r_prow, acst) in (
                    (r_wq, wq_n, pqs, r_pqs, 0), (r_wk, wk_n, pks2, r_pks2, 2)):
                def emit_sum(db, prow=prow, r_prow=r_prow):
                    jb = db % 2
                    p.op(TEN, lambda e, db=db, jb=jb, prow=prow: e.matmul(
                        prow[0:1, 0:512], onescolf[:, :], qbf[:, jb * 512:(jb + 1) * 512],
                        start=(db == 0), stop=(db == 7), skip_group_check=True),
                        reads=[r_qbf[jb], r_consts], writes=[r_prow], deps=pf)
                for db in range(8):
                    reg = db % 4

                    def mkp(e, db=db, reg=reg, wbuf=wbuf):
                        mm = None
                        for k in range(8):
                            mm = e.matmul(
                                pool3[reg][:, :],
                                wbuf[:, k * 1024 + db * 128: k * 1024 + (db + 1) * 128],
                                h_sb[:, k * 512:(k + 1) * 512],
                                start=(k == 0), stop=(k == 7))
                        return mm
                    p.op(TEN, mkp, reads=[r_h[0], r_h[1], r_h[2], r_h[3], wres],
                         writes=[r_pool3[reg]], deps=pf)
                    jb = db % 2
                    p.op(SCA, lambda e, db=db, reg=reg, jb=jb, acst=acst: e.activation(
                        qbf[:, jb * 512:(jb + 1) * 512], pool3[reg][:, :], AF.Sigmoid,
                        bias=affc[:, (acst + 1) * 8 + db: (acst + 1) * 8 + db + 1],
                        scale=affc[:, acst * 8 + db: acst * 8 + db + 1]),
                        reads=[r_pool3[reg], r_affc], writes=[r_qbf[jb]])
                    if db >= 1:
                        emit_sum(db - 1)
                emit_sum(7)
            p.op(VEC, lambda e, c=c: e.tensor_scalar(
                tq_row[0:1, 0:512], pqs[0:1, 0:512],
                -1.0 / D, 1.0, ALU.mult, op1=ALU.add),
                reads=[r_pqs, r_tq], writes=[r_tq])
            p.dma(DY, tq_scr[0:1, c * 512:(c + 1) * 512], tq_row[0:1, 0:512],
                  reads=[r_tq], writes=[r_tqs[c]])
            p.op(VEC, lambda e: e.tensor_scalar(
                tkb_row[0:1, 0:512], pks2[0:1, 0:512],
                -1.0 / D, 1.0, ALU.mult, op1=ALU.add),
                reads=[r_pks2], writes=[r_tkb])
            for j4 in range(4):
                reg4 = j4 % 4
                p.op(TEN, lambda e, j4=j4, reg4=reg4: e.matmul(
                    pool3[reg4][:, 0:1], tkb_row[0:1, j4 * 128:(j4 + 1) * 128],
                    ones1r[0:1, 0:1], start=True, stop=True),
                    reads=[r_tkb, r_consts], writes=[r_pool3[reg4]], deps=pf)
                p.op(SCA, lambda e, c=c, j4=j4, reg4=reg4: e.copy(
                    tkcol[:, c * 4 + j4: c * 4 + j4 + 1], pool3[reg4][:, 0:1]),
                    reads=[r_pool3[reg4]], writes=[r_tkc])
            def emit_vtr(db, c=c):
                jb = db % 2
                for j4 in range(4):
                    jj = j4 % 2
                    p.op(TEN, lambda e, db=db, jb=jb, j4=j4, jj=jj: e.transpose(
                        ptr[jj][:, :], vtp[:, jb * 512 + j4 * 128: jb * 512 + (j4 + 1) * 128],
                        ident[:, :]),
                        reads=[r_vtp[jb]], writes=[r_ptr[jj]], deps=pf)
                    p.op(SCA, lambda e, c=c, db=db, j4=j4, jj=jj: e.copy(
                        v_tm[:, (c * 4 + j4) * D + db * 128: (c * 4 + j4) * D + (db + 1) * 128],
                        ptr[jj][:, :]),
                        reads=[r_ptr[jj]], writes=[r_vtm[c * 4 + j4]])
            for db in range(8):
                reg = db % 4

                def mkvf(e, db=db, reg=reg):
                    mm = None
                    for k in range(8):
                        mm = e.matmul(
                            pool3[reg][:, :],
                            wv_n[:, k * 1024 + db * 128: k * 1024 + (db + 1) * 128],
                            h_sb[:, k * 512:(k + 1) * 512],
                            start=(k == 0), stop=(k == 7))
                    return mm
                p.op(TEN, mkvf, reads=[r_h[0], r_h[1], r_h[2], r_h[3], r_wv],
                     writes=[r_pool3[reg]], deps=pf)
                jb = db % 2
                p.op(SCA, lambda e, db=db, reg=reg, jb=jb: e.activation(
                    vtp[:, jb * 512:(jb + 1) * 512], pool3[reg][:, :], AF.Sigmoid,
                    bias=affc[:, 5 * 8 + db: 5 * 8 + db + 1],
                    scale=affc[:, 4 * 8 + db: 4 * 8 + db + 1]),
                    reads=[r_pool3[reg], r_affc], writes=[r_vtp[jb]])
                if db >= 1:
                    emit_vtr(db - 1)
            emit_vtr(7)

    p.dma(DG, wq_n[:, :].rearrange("p (k i) -> p k i", k=8),
          lay["Wo"].rearrange("(k q) i -> q k i", q=128), writes=[r_wq])

    with ExitStack() as p2:
        at_sb = p2.enter_context(nc.sbuf_tensor(f"{name}_at", [128, 2 * 32 * 512], BF16))
        o_sb = p2.enter_context(nc.sbuf_tensor(f"{name}_os", [128, 8 * 512], BF16))
        tqb = p2.enter_context(nc.sbuf_tensor(f"{name}_tqb", [128, 512], F32))
        dat = p2.enter_context(nc.sbuf_tensor(f"{name}_dat", [128, 2 * 512], F32))
        dex = p2.enter_context(nc.sbuf_tensor(f"{name}_dex", [128, 2 * 512], F32))
        dm2 = p2.enter_context(nc.sbuf_tensor(f"{name}_dm2", [128, 2 * 512], F32))
        zt = p2.enter_context(nc.sbuf_tensor(f"{name}_zt", [128, 512], F32))
        pO = [p2.enter_context(nc.psum_tensor(f"{name}_pO{i}", [128, 512], F32))
              for i in range(8)]
        r_pO = [Res() for _ in range(8)]
        r_datp = [Res(), Res()]
        pf2 = p.fence()

        def at_build_pre(c):
            p.dma(DY, tq_row[0:1, 0:512], tq_scr[0:1, c * 512:(c + 1) * 512],
                  reads=[], writes=[r_tq], deps=[r_tqs[c].w])
            p.op(TEN, lambda e: e.matmul(
                pO[0][:, :], ones1r[0:1, :], tq_row[0:1, 0:512],
                start=True, stop=True),
                reads=[r_consts, r_tq], writes=[r_pO[0]], deps=pf2)
            p.op(SCA, lambda e: e.copy(tqb[:, :], pO[0][:, :]),
                 reads=[r_pO[0]], writes=[r_tqb])

        atb_pending = [None]

        def at_tail(c):
            bb = c % 2
            s2t = atb_pending[0]
            if s2t is None:
                return
            atb_pending[0] = None
            pe = s2t % 2
            o5 = pe * 512
            p.op(VEC, lambda e, o5=o5, pe=pe: e.tensor_tensor(
                dm2[:, o5:o5 + 512], dm2[:, o5:o5 + 512], dex[:, o5:o5 + 512],
                ALU.mult),
                reads=[r_datp[pe]], writes=[r_datp[pe]])
            p.op(VEC, lambda e, bb=bb, s2t=s2t, o5=o5, pe=pe: e.tensor_scalar(
                at_sb[:, bb * 32 * 512 + s2t * 512: bb * 32 * 512 + (s2t + 1) * 512],
                dm2[:, o5:o5 + 512], -0.9, 0.9, ALU.mult, op1=ALU.add),
                reads=[r_datp[pe]], writes=[r_at[bb]])

        def at_build(c, tile_range=None):
            if tile_range is None:
                tile_range = range(NTB_TOT)
            for s2t in tile_range:
                pe = s2t % 2
                o5 = pe * 512
                p.op(VEC, lambda e, s2t=s2t, o5=o5, pe=pe: e.tensor_scalar(
                    dat[:, o5:o5 + 512], tqb[:, :], tkcol[:, s2t:s2t + 1], None,
                    ALU.subtract),
                    reads=[r_tqb, r_tkc, r_datp[pe]], writes=[r_datp[pe]])
                at_tail(c)
                p.op(SCA, lambda e, o5=o5, pe=pe: e.activation(
                    dm2[:, o5:o5 + 512], dat[:, o5:o5 + 512], AF.Sign),
                    reads=[r_datp[pe]], writes=[r_datp[pe]])
                p.op(SCA, lambda e, o5=o5, pe=pe: e.activation(
                    dex[:, o5:o5 + 512], dat[:, o5:o5 + 512], AF.Abs),
                    reads=[r_datp[pe]], writes=[r_datp[pe]])
                p.op(SCA, lambda e, o5=o5, pe=pe: e.activation(
                    dex[:, o5:o5 + 512], dex[:, o5:o5 + 512], AF.Exp, scale=-2.0),
                    reads=[r_datp[pe]], writes=[r_datp[pe]])
                atb_pending[0] = s2t
                if s2t == NTB_TOT - 1:
                    at_tail(c)

        at_build_pre(0)
        at_build(0)
        for c in range(NCH):
            bb = c % 2
            if c + 1 < NCH:
                at_build_pre(c + 1)
            for eb in range(8):
                reg = eb

                def mkO(e, eb=eb, reg=reg, bb=bb):
                    mm = None
                    for s2t in range(NTB_TOT):
                        mm = e.matmul(
                            pO[reg][:, :],
                            v_tm[:, s2t * D + eb * 128: s2t * D + (eb + 1) * 128],
                            at_sb[:, bb * 32 * 512 + s2t * 512: bb * 32 * 512 + (s2t + 1) * 512],
                            start=(s2t == 0), stop=(s2t == NTB_TOT - 1))
                    return mm
                p.op(TEN, mkO, reads=[r_at[bb]] + r_vtm, writes=[r_pO[reg]], deps=pf2)
                p.op(SCA, lambda e, eb=eb, reg=reg: e.copy(
                    o_sb[:, eb * 512:(eb + 1) * 512], pO[reg][:, :]),
                    reads=[r_pO[reg]], writes=[r_os[eb]])
                if c + 1 < NCH:
                    n_per = NTB_TOT // 8
                    at_build(c + 1, range(eb * n_per, (eb + 1) * n_per))
            p.dma(DY, xa[:, :].rearrange("p (tb d) -> p tb d", d=D),
                  xin[c * 512:(c + 1) * 512, :].rearrange("(tb q) d -> q tb d", q=128),
                  writes=r_xa)
            for tb in range(4):
                b0 = 4 + (tb % 2) * 2
                regs = [r_pO[b0], r_pO[b0 + 1]]

                def mko(e, tb=tb, b0=b0):
                    mm = None
                    for n in range(2):
                        for k in range(8):
                            mm = e.matmul(
                                pO[b0 + n][:, :],
                                o_sb[:, k * 512 + tb * 128: k * 512 + (tb + 1) * 128],
                                wq_n[:, k * 1024 + n * 512: k * 1024 + (n + 1) * 512],
                                start=(k == 0), stop=(k == 7))
                    return mm
                p.op(TEN, mko, reads=[r_os[i] for i in range(8)] + [r_wq],
                     writes=regs, deps=pf2)
                for n in range(2):
                    p.op(VEC, lambda e, tb=tb, b0=b0, n=n: e.tensor_tensor(
                        otmp[:, (tb % 2) * D + n * 512: (tb % 2) * D + (n + 1) * 512],
                        pO[b0 + n][:, :], bo_b[:, n * 512:(n + 1) * 512], ALU.add),
                        reads=[regs[n], r_consts], writes=[r_otmp[tb % 2]])
                p.op(VEC, lambda e, tb=tb: e.tensor_tensor(
                    xa[:, tb * D:(tb + 1) * D], xa[:, tb * D:(tb + 1) * D],
                    otmp[:, (tb % 2) * D:(tb % 2 + 1) * D], ALU.add),
                    reads=[r_otmp[tb % 2]], writes=[r_xa[tb]])
            p.dma(DY, xout[c * 512:(c + 1) * 512, :].rearrange("(tb q) d -> q tb d", q=128),
                  xa[:, :].rearrange("p (tb d) -> p tb d", d=D),
                  reads=r_xa)

    return p.fence()



def build_program(TC=4096, subs=(1, 2, 3, 4, 5, 6)):
    nc = bass.Bass("TRN2", target_bir_lowering=False, debug=False)

    inp = {}

    def din(nm, shape, dt=F32):
        inp[nm] = nc.dram_tensor(nm, list(shape), dt, kind="ExternalInput")
        return inp[nm]

    x_in = din("x", (TC, D))
    din("L0_ffn_gate_weight", (32, D))
    din("L0_ffn_w_gate", (32, D, 512))
    din("L0_ffn_w_up", (32, D, 512))
    din("L0_ffn_w_down", (32, 512, D))
    din("L0_ffn_shared_w_gate", (D, 512))
    din("L0_ffn_shared_w_up", (D, 512))
    din("L0_ffn_shared_w_down", (512, D))
    din("L0_ffn_anchor_w_gate", (16, D, 256))
    din("L0_ffn_anchor_w_up", (16, D, 256))
    din("L0_ffn_anchor_w_down", (16, 256, D))
    din("gbb0", (128, 32))
    din("L1_ffn_gate_kernel", (D, 16))
    din("L1_ffn_w_gate", (16, D, 512))
    din("L1_ffn_w_up", (16, D, 512))
    din("L1_ffn_w_down", (16, 512, D))
    din("L1_ffn_sw_gate", (D, D))
    din("L1_ffn_sw_up", (D, D))
    din("L1_ffn_sw_down", (D, D))
    din("gbb1", (128, 16))
    din("ident", (128, 128), BF16)
    din("L0_attn_wq", (D, D))
    din("L0_attn_wk", (D, D))
    din("L0_attn_wv", (D, D))
    din("L0_attn_wo", (D, D))
    din("bob0", (128, D))
    din("L1_attn_Wq", (D, D))
    din("L1_attn_Wk", (D, D))
    din("L1_attn_Wv", (D, D))
    din("L1_attn_Wo", (D, D))
    din("bob1", (128, D))
    din("blockones", (128, 128), BF16)
    din("ekpat", (16, 1024), BF16)
    din("onescol", (128, 1), BF16)
    din("onescolf", (128, 1))
    din("ones1r", (1, 128))
    AFF_NAMES = ["gq", "bq", "mq", "vq", "gk", "bk", "mk", "vk",
                 "gv", "bv", "mv", "vv"]
    for nm in AFF_NAMES:
        din(f"aff_{nm}", (128, 8))

    x_scr = nc.dram_tensor("x_scr", [TC, D], F32)
    y_out = nc.dram_tensor("y", [TC, D], F32, kind="ExternalOutput")

    with ExitStack() as st:
        p = Prog(nc, st)
        blk = st.enter_context(nc.Block())
        ident = st.enter_context(nc.sbuf_tensor("ident_sb", [128, 128], BF16))
        r_ident = Res()
        p.dma(DG, ident[:, :], inp["ident"][:, :], writes=[r_ident])
        fence = p.fence()

        def l0_units():
            us = [(inp["L0_ffn_shared_w_gate"][:, :], inp["L0_ffn_shared_w_up"][:, :],
                   inp["L0_ffn_shared_w_down"][:, :], 512, "const", 0)]
            for e in range(32):
                us.append((inp["L0_ffn_w_gate"][e, :, :], inp["L0_ffn_w_up"][e, :, :],
                           inp["L0_ffn_w_down"][e, :, :], 512, "cw", e))
            for l in range(16):
                us.append((inp["L0_ffn_anchor_w_gate"][l, :, :],
                           inp["L0_ffn_anchor_w_up"][l, :, :],
                           inp["L0_ffn_anchor_w_down"][l, :, :], 256, "acw", l))
            return us

        def l1_units():
            us = []
            for h2 in range(2):
                us.append((inp["L1_ffn_sw_gate"][:, h2 * 512:(h2 + 1) * 512],
                           inp["L1_ffn_sw_up"][:, h2 * 512:(h2 + 1) * 512],
                           inp["L1_ffn_sw_down"][h2 * 512:(h2 + 1) * 512, :],
                           512, "const", 0))
            for e in range(16):
                us.append((inp["L1_ffn_w_gate"][e, :, :], inp["L1_ffn_w_up"][e, :, :],
                           inp["L1_ffn_w_down"][e, :, :], 512, "cw", e))
            return us

        lay_l0 = dict(E=32, n_group=4, top_k=6, route_scale=0.2,
                      gate="l0", gwT_src=("transpose", inp["L0_ffn_gate_weight"][:, :]),
                      gb_bcast=inp["gbb0"][:, :], units=l0_units(), ident=ident)
        lay_l1 = dict(E=16, n_group=4, top_k=4, route_scale=2.5,
                      gate="l1", gwT_src=("direct", inp["L1_ffn_gate_kernel"][:, :]),
                      gb_bcast=inp["gbb1"][:, :], units=l1_units(), ident=ident)

        lay_a0 = dict(name="a0", wq=inp["L0_attn_wq"][:, :], wk=inp["L0_attn_wk"][:, :],
                      wv=inp["L0_attn_wv"][:, :], wo=inp["L0_attn_wo"][:, :],
                      bo_bcast=inp["bob0"][:, :], blockones=inp["blockones"][:, :],
                      ekpat=inp["ekpat"][:, :], onescol=inp["onescol"][:, :],
                      ident=ident)
        lay_a1 = dict(name="a1", Wq=inp["L1_attn_Wq"][:, :], Wk=inp["L1_attn_Wk"][:, :],
                      Wv=inp["L1_attn_Wv"][:, :], Wo=inp["L1_attn_Wo"][:, :],
                      bo_bcast=inp["bob1"][:, :], onescol=inp["onescol"][:, :],
                      onescolf=inp["onescolf"][:, :], ones1r=inp["ones1r"][:, :],
                      aff_cols=[inp[f"aff_{nm}"][:, :] for nm in AFF_NAMES],
                      ident=ident)

        seq = []
        if 1 in subs:
            seq.append(("ffn", dict(lay_l0, name="f0a")))
        if 2 in subs:
            seq.append(("attn0", lay_a0))
        if 3 in subs:
            seq.append(("ffn", dict(lay_l0, name="f0b")))
        if 4 in subs:
            seq.append(("ffn", dict(lay_l1, name="f1a")))
        if 5 in subs:
            seq.append(("attn1", lay_a1))
        if 6 in subs:
            seq.append(("ffn", dict(lay_l1, name="f1b")))

        cur = x_in
        for i, (kindl, lay) in enumerate(seq):
            last = (i == len(seq) - 1)
            xout = y_out if last else x_scr
            scope_name = f"s{i+1}_{kindl}_{lay['name'] if lay else ''}"
            with ExitStack() as sub_st:
                with nc.named_scope(scope_name):
                    if kindl == "ffn":
                        fence = emit_ffn(p, sub_st, fence, lay, cur, xout, TC)
                    elif kindl == "attn0":
                        fence = emit_attn0(p, sub_st, fence, lay, cur, xout, TC)
                    else:
                        fence = emit_attn1(p, sub_st, fence, lay, cur, xout, TC)
            cur = xout

        for q in (DY, DG):
            for i in range(NSLOT):
                nm = f"{q}{i}"
                if p.cnt[nm] > 0:
                    nc.sync.wait_ge(p.sem[nm], p.cnt[nm])

    return nc


WEIGHT_KEYS = ["L0_ffn_gate_weight", "L0_ffn_w_gate", "L0_ffn_w_up", "L0_ffn_w_down",
               "L0_ffn_shared_w_gate", "L0_ffn_shared_w_up", "L0_ffn_shared_w_down",
               "L0_ffn_anchor_w_gate", "L0_ffn_anchor_w_up", "L0_ffn_anchor_w_down",
               "L1_ffn_gate_kernel", "L1_ffn_w_gate", "L1_ffn_w_up", "L1_ffn_w_down",
               "L1_ffn_sw_gate", "L1_ffn_sw_up", "L1_ffn_sw_down"]


def make_in_maps(inputs, n_cores=8):
    x = np.ascontiguousarray(np.asarray(inputs["L0_attn_x"], dtype=np.float32))
    base = {}
    for k in WEIGHT_KEYS:
        base[k] = np.ascontiguousarray(np.asarray(inputs[k], dtype=np.float32))
    base["gbb0"] = np.ascontiguousarray(np.broadcast_to(
        np.asarray(inputs["L0_ffn_gate_bias"], dtype=np.float32)[None, :], (128, 32)))
    base["gbb1"] = np.ascontiguousarray(np.broadcast_to(
        np.asarray(inputs["L1_ffn_gate_bias"], dtype=np.float32)[None, :], (128, 16)))
    base["ident"] = np.eye(128, dtype=ml_dtypes.bfloat16)
    for k in ["L0_attn_wq", "L0_attn_wk", "L0_attn_wv", "L0_attn_wo",
              "L1_attn_Wq", "L1_attn_Wk", "L1_attn_Wv", "L1_attn_Wo"]:
        base[k] = np.ascontiguousarray(np.asarray(inputs[k], dtype=np.float32))
    base["bob0"] = np.ascontiguousarray(np.broadcast_to(
        np.asarray(inputs["L0_attn_bo"], dtype=np.float32)[None, :], (128, D)))
    base["bob1"] = np.ascontiguousarray(np.broadcast_to(
        np.asarray(inputs["L1_attn_bo"], dtype=np.float32)[None, :], (128, D)))
    blk = np.zeros((128, 128), dtype=ml_dtypes.bfloat16)
    for db in range(8):
        for r in range(128):
            blk[r, db * 16 + 2 * db + r // 64] = 1
    base["blockones"] = blk
    ekp = np.zeros((16, 1024), dtype=ml_dtypes.bfloat16)
    for db in range(8):
        for r in range(128):
            ekp[2 * db + r // 64, db * 128 + r] = 1
    base["ekpat"] = ekp
    base["onescol"] = np.ones((128, 1), dtype=ml_dtypes.bfloat16)
    base["onescolf"] = np.ones((128, 1), dtype=np.float32)
    base["ones1r"] = np.ones((1, 128), dtype=np.float32)
    for nm in ["gq", "bq", "mq", "vq", "gk", "bk", "mk", "vk", "gv", "bv", "mv", "vv"]:
        v = np.asarray(inputs[f"L1_attn_{nm}"], dtype=np.float32)
        base[f"aff_{nm}"] = np.ascontiguousarray(v.reshape(8, 128).T)
    maps = []
    for c in range(n_cores):
        m = dict(base)
        m["x"] = np.ascontiguousarray(x[c])
        maps.append(m)
    return maps


def kernel(**inputs):
    nc = build_program(TC=4096, subs=(1, 2, 3, 4, 5, 6))
    maps = make_in_maps(inputs, 8)
    res = run_bass_kernel_spmd(nc, maps, core_ids=list(range(8)))
    out = np.stack([res.results[i]["y"] for i in range(8)], axis=0)
    return out.astype(np.float32)


# version 12 (final)
# speedup vs baseline: 1.0844x; 1.0844x over previous
import numpy as np
import ml_dtypes
import concourse.bass as bass
import concourse.mybir as mybir
from concourse.bass_utils import run_bass_kernel_spmd
from contextlib import ExitStack

F32 = mybir.dt.float32
BF16 = mybir.dt.bfloat16
AF = mybir.ActivationFunctionType
ALU = mybir.AluOpType
AX = mybir.AxisListType

D = 1024
BIG = 1e30

TEN, VEC, SCA, GPS, SYN = "tsem", "vsem", "ssem", "gsem", "ysem"
DG, DY = "dg", "dy"
ENGINES = [TEN, VEC, SCA, GPS, SYN]


class Res:
    __slots__ = ("w", "rs")

    def __init__(self):
        self.w = None
        self.rs = []


NSLOT = 10


class Prog:
    def __init__(self, nc, st):
        self.nc = nc
        self.sem = {}
        self.cnt = {}
        for name in ENGINES:
            self.sem[name] = st.enter_context(nc.semaphore(name))
            self.cnt[name] = 0
        self.dslot = {}
        for q in (DG, DY):
            for i in range(NSLOT):
                nm = f"{q}{i}"
                self.sem[nm] = st.enter_context(nc.semaphore(nm))
                self.cnt[nm] = 0
            self.dslot[q] = 0
        self.eng = {TEN: nc.tensor, VEC: nc.vector, SCA: nc.scalar,
                    GPS: nc.gpsimd, SYN: nc.sync}
        self.dma_eng = {DG: nc.gpsimd, DY: nc.sync}
        self.dma_issuer = {DG: GPS, DY: SYN}
        self.waited = {k: {} for k in ENGINES}

    def _wait(self, issuer_key, eng_obj, sem_name, val):
        if val <= 0:
            return
        w = self.waited[issuer_key]
        if w.get(sem_name, 0) >= val:
            return
        eng_obj.wait_ge(self.sem[sem_name], val)
        w[sem_name] = val

    def _dep_tok(self, issuer_key, eng_obj, tok):
        if tok is not None:
            self._wait(issuer_key, eng_obj, tok[0], tok[1])

    def _deps(self, issuer_key, eng_obj, reads, writes):
        for res in reads:
            self._dep_tok(issuer_key, eng_obj, res.w)
        for res in writes:
            if res.w is not None and res.w[0] != issuer_key:
                self._dep_tok(issuer_key, eng_obj, res.w)
            for tok in res.rs:
                if tok[0] != issuer_key:
                    self._dep_tok(issuer_key, eng_obj, tok)

    def op(self, ekey, fn, reads=(), writes=(), deps=()):
        eng = self.eng[ekey]
        for tok in deps:
            self._dep_tok(ekey, eng, tok)
        self._deps(ekey, eng, reads, writes)
        inst = fn(eng)
        self.cnt[ekey] += 1
        val = self.cnt[ekey]
        inst.then_inc(self.sem[ekey], 1)
        tok = (ekey, val)
        for res in writes:
            res.w = tok
            res.rs = []
        for res in reads:
            res.rs.append(tok)
        return tok

    def dma(self, qkey, out_ap, in_ap, reads=(), writes=(), deps=()):
        eng = self.dma_eng[qkey]
        ikey = self.dma_issuer[qkey]
        slot = self.dslot[qkey]
        self.dslot[qkey] = (slot + 1) % NSLOT
        snm = f"{qkey}{slot}"
        self._wait(ikey, eng, snm, self.cnt[snm])
        for tok in deps:
            self._dep_tok(ikey, eng, tok)
        self._deps(ikey, eng, reads, writes)
        inst = eng.dma_start(out_ap, in_ap)
        self.cnt[snm] += 16
        val = self.cnt[snm]
        inst.then_inc(self.sem[snm], 16)
        tok = (snm, val)
        for res in writes:
            res.w = tok
            res.rs = []
        for res in reads:
            res.rs.append(tok)
        return tok

    def fence(self):
        return [(k, v) for k, v in self.cnt.items() if v > 0]

    def barrier(self, deps):
        for ekey in ENGINES:
            eng = self.eng[ekey]
            for tok in deps:
                self._dep_tok(ekey, eng, tok)


def bview(t, offset, dims):
    return bass.AP(t, offset, dims)



def emit_ffn(p, st, fence, lay, xin, xout, TC):
    nc = p.nc
    name = lay["name"]
    E = lay["E"]
    G = lay["n_group"]
    EG = E // G
    K = lay["top_k"]
    TH = TC // 2
    NCH = TH // 512
    NTB = TH // 128
    NE = NTB * E
    NG = NTB * G

    p.barrier(fence)

    sb = st.enter_context
    acc = sb(nc.sbuf_tensor(f"{name}_acc", [128, NTB * D], F32))
    h_b = sb(nc.sbuf_tensor(f"{name}_h", [128, 8 * TH], BF16))
    lnb = sb(nc.sbuf_tensor(f"{name}_lnb", [128, 4 * D], BF16))
    stat = sb(nc.sbuf_tensor(f"{name}_stat", [128, 48], F32))
    sgate = sb(nc.sbuf_tensor(f"{name}_sg", [128, NE], F32))
    scb = sb(nc.sbuf_tensor(f"{name}_scb", [128, NE], F32))
    mwork = sb(nc.sbuf_tensor(f"{name}_mw", [128, NE], F32))
    gtmp = sb(nc.sbuf_tensor(f"{name}_gt", [128, NE], F32))
    e1b = sb(nc.sbuf_tensor(f"{name}_e1", [128, NE], F32))
    mjs = sb(nc.sbuf_tensor(f"{name}_mjs", [128, NTB * 20], F32))
    cwb = sb(nc.sbuf_tensor(f"{name}_cw", [128, NE], F32))
    acwb = sb(nc.sbuf_tensor(f"{name}_acw", [128, NTB * 16], F32))
    gwT = sb(nc.sbuf_tensor(f"{name}_gwT", [128, 8 * E], BF16))
    gb_b = sb(nc.sbuf_tensor(f"{name}_gbb", [128, E], F32))
    gnat = sb(nc.sbuf_tensor(f"{name}_gnat", [128, 1024], BF16))
    wg_b = sb(nc.sbuf_tensor(f"{name}_wg", [128, 2 * 4096], BF16))
    wu_b = sb(nc.sbuf_tensor(f"{name}_wu", [128, 2 * 4096], BF16))
    wd_b = sb(nc.sbuf_tensor(f"{name}_wd", [128, 2 * 4096], BF16))
    silu_s = sb(nc.sbuf_tensor(f"{name}_sil", [128, 2048], BF16))
    inter_s = sb(nc.sbuf_tensor(f"{name}_int", [128, 2 * 2048], BF16))
    tmp_s = sb(nc.sbuf_tensor(f"{name}_tmp", [128, 2 * D], F32))
    sqd = tmp_s

    units = lay["units"]
    NU = len(units)

    r_acc = [Res() for _ in range(NTB)]
    r_hcol = [Res() for _ in range(NTB)]
    r_lnb = [Res() for _ in range(4)]
    r_stat = Res()
    r_sq = [Res() for _ in range(4)]
    r_sqd = Res()
    _rs = Res()
    r_sgate = [_rs, _rs]
    r_gate = Res()
    _rc = Res()
    r_cw = [_rc, _rc]
    r_gwT = Res()
    r_gnat = Res()
    r_gbb = Res()
    r_wg = [Res() for _ in range(2)]
    r_wu = [Res() for _ in range(2)]
    r_wd = [Res() for _ in range(2)]
    r_silu = [Res() for _ in range(4)]
    r_inter = [[Res() for _ in range(4)] for _ in range(2)]
    r_tmp = [Res() for _ in range(2)]

    ident = lay["ident"]

    def load_unit(u):
        wg_ap, wu_ap, wd_ap, I_u, _, _ = units[u]
        b = u % 2
        p.dma(DG, wg_b[:, b * 4096: b * 4096 + 8 * I_u].rearrange("p (k i) -> p k i", k=8),
              wg_ap.rearrange("(k q) i -> q k i", q=128), writes=[r_wg[b]])
        p.dma(DG, wu_b[:, b * 4096: b * 4096 + 8 * I_u].rearrange("p (k i) -> p k i", k=8),
              wu_ap.rearrange("(k q) i -> q k i", q=128), writes=[r_wu[b]])
        p.dma(DG, wd_b[:, b * 4096: b * 4096 + (I_u // 128) * D].rearrange("p (k j) -> p k j", j=D),
              wd_ap.rearrange("(k q) j -> q k j", q=128), writes=[r_wd[b]])

    kind, gw_ap = lay["gwT_src"]
    if kind == "direct":
        p.dma(DG, gwT[:, :].rearrange("p (k e) -> p k e", k=8),
              gw_ap.rearrange("(k q) e -> q k e", q=128), writes=[r_gwT])
    else:
        p.dma(DG, gnat[0:E, 0:1024], gw_ap, writes=[r_gnat])
    p.dma(DG, gb_b[:, :], lay["gb_bcast"], writes=[r_gbb])

    for hh in range(2):
        tok0 = hh * TH

        with ExitStack() as pa:
            ptr = [pa.enter_context(nc.psum_tensor(f"{name}_pt{hh}{i}", [128, 128], BF16))
                   for i in range(2)]
            plog = pa.enter_context(nc.psum_tensor(f"{name}_pl{hh}", [128, 512], F32))
            r_ptr = [Res(), Res()]
            r_plog = Res()
            pfence = p.fence()

            if hh == 0 and kind == "transpose":
                for k in range(8):
                    p.op(TEN, lambda e, k=k: e.transpose(
                        ptr[k % 2][0:128, 0:E], gnat[0:E, k * 128:(k + 1) * 128],
                        ident[0:E, 0:E]),
                        reads=[r_gnat], writes=[r_ptr[k % 2]], deps=pfence)
                    p.op(SCA, lambda e, k=k: e.copy(gwT[:, k * E:(k + 1) * E],
                                                    ptr[k % 2][0:128, 0:E]),
                         reads=[r_ptr[k % 2]], writes=[r_gwT])

            if hh == 0:
                load_unit(0)
                if NU > 1:
                    load_unit(1)

            for c in range(NCH):
                p.dma(DY,
                      acc[:, :].rearrange("p (tb d) -> p tb d", d=D)[:, c * 4:(c + 1) * 4, :],
                      xin[tok0 + c * 512: tok0 + (c + 1) * 512, :].rearrange(
                          "(tb q) d -> q tb d", q=128),
                      writes=r_acc[c * 4:c * 4 + 4])
                p.op(VEC, lambda e, c=c: e.tensor_reduce(
                    stat[:, 0:4].rearrange("p (a o) -> p a o", o=1),
                    acc[:, :].rearrange("p (tb d) -> p tb d", d=D)[:, c * 4:(c + 1) * 4, :],
                    AX.X, ALU.add),
                    reads=r_acc[c * 4:c * 4 + 4], writes=[r_stat])
                for tb in range(4):
                    p.op(SCA, lambda e, c=c, tb=tb: e.activation(
                        sqd[:, 0:D], acc[:, (c * 4 + tb) * D:(c * 4 + tb + 1) * D],
                        AF.Square, accum_out=stat[:, 8 + tb: 9 + tb]),
                        reads=[r_acc[c * 4 + tb], r_sqd], writes=[r_sq[tb], r_sqd, r_tmp[0]])
                p.op(VEC, lambda e: e.tensor_scalar(stat[:, 0:4], stat[:, 0:4],
                                                    1.0 / D, None, ALU.mult),
                     reads=[r_stat], writes=[r_stat])
                p.op(VEC, lambda e: e.tensor_scalar(stat[:, 12:16], stat[:, 8:12],
                                                    1.0 / D, None, ALU.mult),
                     reads=[r_sq[0], r_sq[1], r_sq[2], r_sq[3]], writes=[r_stat])
                p.op(VEC, lambda e: e.tensor_tensor(stat[:, 20:24], stat[:, 0:4],
                                                    stat[:, 0:4], ALU.mult),
                     reads=[r_stat], writes=[r_stat])
                p.op(VEC, lambda e: e.tensor_tensor(stat[:, 12:16], stat[:, 12:16],
                                                    stat[:, 20:24], ALU.subtract),
                     reads=[r_stat], writes=[r_stat])
                p.op(VEC, lambda e: e.tensor_scalar(stat[:, 12:16], stat[:, 12:16],
                                                    1e-5, None, ALU.add),
                     reads=[r_stat], writes=[r_stat])
                p.op(SCA, lambda e: e.activation(stat[:, 24:28], stat[:, 12:16], AF.Sqrt),
                     reads=[r_stat], writes=[r_stat])
                p.op(VEC, lambda e: e.reciprocal(stat[:, 28:32], stat[:, 24:28]),
                     reads=[r_stat], writes=[r_stat])
                for tb in range(4):
                    p.op(VEC, lambda e, c=c, tb=tb: e.tensor_scalar(
                        lnb[:, tb * D:(tb + 1) * D],
                        acc[:, (c * 4 + tb) * D:(c * 4 + tb + 1) * D],
                        stat[:, tb: tb + 1], stat[:, 28 + tb: 29 + tb],
                        ALU.subtract, op1=ALU.mult),
                        reads=[r_acc[c * 4 + tb], r_stat], writes=[r_lnb[tb]])
                    for k in range(8):
                        p.op(TEN, lambda e, tb=tb, k=k: e.transpose(
                            ptr[k % 2][:, :], lnb[:, tb * D + k * 128: tb * D + (k + 1) * 128],
                            ident[:, :]),
                            reads=[r_lnb[tb]], writes=[r_ptr[k % 2]], deps=pfence)
                        p.op(SCA, lambda e, c=c, tb=tb, k=k: e.copy(
                            h_b[:, k * TH + c * 512 + tb * 128: k * TH + c * 512 + (tb + 1) * 128],
                            ptr[k % 2][:, :]),
                            reads=[r_ptr[k % 2]], writes=[r_hcol[c * 4 + tb]])
                for tb in range(4):
                    def mk(e, c=c, tb=tb):
                        mm = None
                        for k in range(8):
                            mm = e.matmul(
                                plog[:, (c * 4 + tb) * E:(c * 4 + tb + 1) * E],
                                h_b[:, k * TH + c * 512 + tb * 128: k * TH + c * 512 + (tb + 1) * 128],
                                gwT[:, k * E:(k + 1) * E],
                                start=(k == 0), stop=(k == 7))
                        return mm
                    p.op(TEN, mk, reads=[r_hcol[c * 4 + tb], r_gwT], writes=[r_plog],
                         deps=pfence)

            p.op(SCA, lambda e: e.activation(sgate[:, 0:NE],
                                             plog[:, 0:NE], AF.Sigmoid),
                 reads=[r_plog], writes=[r_sgate[hh]])

            sg_ap = sgate[:, 0:NE]
            slot_thr = 2 * NG
            slot_top = 2 * NG + NTB
            slot_den = 2 * NG + 7 * NTB
            slot_inv = 2 * NG + 8 * NTB

            def g_op(fn, extra_reads=(), wr=None):
                p.op(VEC, fn, reads=[r_gate] + list(extra_reads),
                     writes=[wr if wr is not None else r_gate])

            g_op(lambda e: e.tensor_tensor(
                scb[:, :].rearrange("p (t q) -> p t q", q=E),
                sg_ap.rearrange("p (t q) -> p t q", q=E),
                bview(gb_b, 0, [[E, 128], [0, NTB], [1, E]]), ALU.add),
                extra_reads=[r_sgate[hh], r_gbb])
            sc4 = scb[:, :].rearrange("p (tg q) -> p tg q", q=EG)
            g_op(lambda e: e.tensor_reduce(
                mjs[:, 0:NG].rearrange("p (a o) -> p a o", o=1), sc4, AX.X, ALU.max))
            g_op(lambda e: e.tensor_tensor(
                gtmp[:, :].rearrange("p (tg q) -> p tg q", q=EG), sc4,
                bview(mjs, 0, [[NTB * 20, 128], [1, NG], [0, EG]]), ALU.is_ge))
            g_op(lambda e: e.tensor_scalar(gtmp[:, :], gtmp[:, :], BIG, None, ALU.mult))
            g_op(lambda e: e.tensor_tensor(mwork[:, :], scb[:, :], gtmp[:, :], ALU.subtract))
            g_op(lambda e: e.tensor_reduce(
                mjs[:, NG:2 * NG].rearrange("p (a o) -> p a o", o=1),
                mwork[:, :].rearrange("p (tg q) -> p tg q", q=EG), AX.X, ALU.max))
            g_op(lambda e: e.tensor_tensor(mjs[:, 0:NG], mjs[:, 0:NG],
                                           mjs[:, NG:2 * NG], ALU.add))
            gs3 = mjs[:, 0:NG].rearrange("p (t g) -> p t g", g=G)
            g_op(lambda e: e.tensor_reduce(
                mjs[:, slot_thr:slot_thr + NTB].rearrange("p (a o) -> p a o", o=1),
                gs3, AX.X, ALU.max))
            g_op(lambda e: e.tensor_tensor(
                gtmp[:, 0:NG].rearrange("p (t g) -> p t g", g=G), gs3,
                bview(mjs, slot_thr, [[NTB * 20, 128], [1, NTB], [0, G]]), ALU.is_ge))
            g_op(lambda e: e.tensor_scalar(gtmp[:, 0:NG], gtmp[:, 0:NG], BIG, None, ALU.mult))
            g_op(lambda e: e.tensor_tensor(gtmp[:, NG:2 * NG], mjs[:, 0:NG],
                                           gtmp[:, 0:NG], ALU.subtract))
            g_op(lambda e: e.tensor_reduce(
                mjs[:, slot_thr:slot_thr + NTB].rearrange("p (a o) -> p a o", o=1),
                gtmp[:, NG:2 * NG].rearrange("p (t g) -> p t g", g=G), AX.X, ALU.max))
            g_op(lambda e: e.tensor_tensor(
                gtmp[:, 0:NG].rearrange("p (t g) -> p t g", g=G), gs3,
                bview(mjs, slot_thr, [[NTB * 20, 128], [1, NTB], [0, G]]), ALU.is_ge))
            g_op(lambda e: e.tensor_scalar(e1b[:, :], scb[:, :], 0.0, None, ALU.mult))
            g_op(lambda e: e.tensor_tensor(
                e1b[:, :].rearrange("p (tg q) -> p tg q", q=EG),
                e1b[:, :].rearrange("p (tg q) -> p tg q", q=EG),
                bview(gtmp, 0, [[NE, 128], [1, NG], [0, EG]]), ALU.add))
            if lay["gate"] == "l0":
                g_op(lambda e: e.tensor_scalar(gtmp[:, :], e1b[:, :], BIG, BIG,
                                               ALU.mult, op1=ALU.subtract))
                g_op(lambda e: e.tensor_tensor(mwork[:, :], scb[:, :], e1b[:, :], ALU.mult))
                g_op(lambda e: e.tensor_tensor(mwork[:, :], mwork[:, :], gtmp[:, :], ALU.add))
            else:
                g_op(lambda e: e.tensor_tensor(mwork[:, :], scb[:, :], e1b[:, :], ALU.mult))
            g_op(lambda e: e.tensor_copy(gtmp[:, :], mwork[:, :]))
            m3 = mwork[:, :].rearrange("p (t q) -> p t q", q=E)
            for j in range(K):
                g_op(lambda e, j=j: e.tensor_reduce(
                    mjs[:, slot_top + j * NTB: slot_top + (j + 1) * NTB].rearrange(
                        "p (a o) -> p a o", o=1), m3, AX.X, ALU.max))
                if j == 0 and lay["gate"] == "l0":
                    g_op(lambda e: e.tensor_tensor(
                        e1b[:, :].rearrange("p (t q) -> p t q", q=E), m3,
                        bview(mjs, slot_top, [[NTB * 20, 128], [1, NTB], [0, E]]),
                        ALU.is_ge))
                if j < K - 1:
                    g_op(lambda e, j=j: e.tensor_tensor(
                        scb[:, :].rearrange("p (t q) -> p t q", q=E), m3,
                        bview(mjs, slot_top + j * NTB,
                              [[NTB * 20, 128], [1, NTB], [0, E]]), ALU.is_ge))
                    g_op(lambda e: e.tensor_scalar(scb[:, :], scb[:, :], BIG, None, ALU.mult))
                    g_op(lambda e: e.tensor_tensor(mwork[:, :], mwork[:, :],
                                                   scb[:, :], ALU.subtract))
            g_op(lambda e: e.tensor_tensor(
                mwork[:, :].rearrange("p (t q) -> p t q", q=E),
                gtmp[:, :].rearrange("p (t q) -> p t q", q=E),
                bview(mjs, slot_top + (K - 1) * NTB,
                      [[NTB * 20, 128], [1, NTB], [0, E]]), ALU.is_ge))
            wsrc = sg_ap if lay["gate"] == "l0" else gtmp[:, :]
            g_op(lambda e: e.tensor_tensor(cwb[:, 0:NE],
                                           mwork[:, :], wsrc, ALU.mult),
                 extra_reads=[r_sgate[hh]])
            g_op(lambda e: e.tensor_reduce(
                mjs[:, slot_den:slot_den + NTB].rearrange("p (a o) -> p a o", o=1),
                cwb[:, 0:NE].rearrange("p (t q) -> p t q", q=E),
                AX.X, ALU.add))
            g_op(lambda e: e.tensor_scalar(mjs[:, slot_den:slot_den + NTB],
                                           mjs[:, slot_den:slot_den + NTB],
                                           1e-20, None, ALU.add))
            g_op(lambda e: e.reciprocal(mjs[:, slot_inv:slot_inv + NTB],
                                        mjs[:, slot_den:slot_den + NTB]))
            g_op(lambda e: e.tensor_scalar(mjs[:, slot_inv:slot_inv + NTB],
                                           mjs[:, slot_inv:slot_inv + NTB],
                                           lay["route_scale"] * 0.5, None, ALU.mult))
            g_op(lambda e: e.tensor_tensor(
                cwb[:, 0:NE].rearrange("p (t q) -> p t q", q=E),
                cwb[:, 0:NE].rearrange("p (t q) -> p t q", q=E),
                bview(mjs, slot_inv, [[NTB * 20, 128], [1, NTB], [0, E]]), ALU.mult),
                wr=r_cw[hh])
            if lay["gate"] == "l0":
                e1p = e1b[:, :].rearrange("p (tl two) -> p tl two", two=2)
                g_op(lambda e: e.tensor_tensor(
                    acwb[:, 0:NTB * 16].rearrange(
                        "p (a o) -> p a o", o=1),
                    e1p[:, :, 0:1], e1p[:, :, 1:2], ALU.add))
                p.op(VEC, lambda e: e.tensor_scalar(
                    acwb[:, 0:NTB * 16],
                    acwb[:, 0:NTB * 16],
                    0.1 * 0.5, None, ALU.mult),
                    reads=[r_gate], writes=[r_cw[hh]])

            fence_a = [("ssem", p.cnt["ssem"])]

        with ExitStack() as pb:
            pg = [pb.enter_context(nc.psum_tensor(f"{name}_pg{hh}{i}", [128, 512], F32))
                  for i in range(4)]
            py = [pb.enter_context(nc.psum_tensor(f"{name}_py{hh}{i}", [128, 1024], F32))
                  for i in range(2)]
            r_pg = [Res() for _ in range(4)]
            r_py = [Res() for _ in range(2)]
            first_pg = [True] * 4
            first_py = [True] * 2

            for u in range(NU):
                wg_ap, wu_ap, wd_ap, I_u, cw_kind, idx = units[u]
                b = u % 2
                MU = I_u // 128
                for c in range(NCH):
                    for m in range(MU):
                        def mkg(e, b=b, m=m, c=c, I_u=I_u):
                            mm = None
                            for k in range(8):
                                mm = e.matmul(
                                    pg[m][:, 0:512],
                                    wg_b[:, b * 4096 + k * I_u + m * 128:
                                         b * 4096 + k * I_u + (m + 1) * 128],
                                    h_b[:, k * TH + c * 512: k * TH + (c + 1) * 512],
                                    start=(k == 0), stop=(k == 7))
                            return mm
                        extra = fence_a if first_pg[m] else []
                        first_pg[m] = False
                        p.op(TEN, mkg,
                             reads=[r_wg[b]] + r_hcol[c * 4: c * 4 + 4],
                             writes=[r_pg[m]], deps=extra)
                    for m in range(MU):
                        p.op(SCA, lambda e, m=m: e.activation(
                            silu_s[:, m * 512:(m + 1) * 512], pg[m][:, 0:512], AF.Silu),
                            reads=[r_pg[m]], writes=[r_silu[m]])
                    for m in range(MU):
                        def mku(e, b=b, m=m, c=c, I_u=I_u):
                            mm = None
                            for k in range(8):
                                mm = e.matmul(
                                    pg[m][:, 0:512],
                                    wu_b[:, b * 4096 + k * I_u + m * 128:
                                         b * 4096 + k * I_u + (m + 1) * 128],
                                    h_b[:, k * TH + c * 512: k * TH + (c + 1) * 512],
                                    start=(k == 0), stop=(k == 7))
                            return mm
                        p.op(TEN, mku,
                             reads=[r_wu[b]] + r_hcol[c * 4: c * 4 + 4],
                             writes=[r_pg[m]])
                    cp = c % 2
                    for m in range(MU):
                        p.op(VEC, lambda e, m=m, cp=cp: e.tensor_tensor(
                            inter_s[:, cp * 2048 + m * 512: cp * 2048 + (m + 1) * 512],
                            silu_s[:, m * 512:(m + 1) * 512], pg[m][:, 0:512], ALU.mult),
                            reads=[r_silu[m], r_pg[m]], writes=[r_inter[cp][m]])
                    for tb in range(4):
                        extra = fence_a if first_py[tb % 2] else []
                        first_py[tb % 2] = False
                        for k in range(MU):
                            def mky(e, b=b, tb=tb, k=k, MU=MU, cp=cp):
                                mm = None
                                for n in range(2):
                                    mm = e.matmul(
                                        py[tb % 2][:, n * 512:(n + 1) * 512],
                                        inter_s[:, cp * 2048 + k * 512 + tb * 128:
                                                cp * 2048 + k * 512 + tb * 128 + 128],
                                        wd_b[:, b * 4096 + k * D + n * 512:
                                             b * 4096 + k * D + (n + 1) * 512],
                                        start=(k == 0), stop=(k == MU - 1))
                                return mm
                            p.op(TEN, mky, reads=[r_inter[cp][k], r_wd[b]],
                                 writes=[r_py[tb % 2]], deps=extra)
                        if cw_kind == "const":
                            p.op(SCA, lambda e, tb=tb: e.activation(
                                tmp_s[:, (tb % 2) * D:(tb % 2 + 1) * D],
                                py[tb % 2][:, 0:1024], AF.Copy, scale=0.5),
                                reads=[r_py[tb % 2]], writes=[r_tmp[tb % 2]])
                        elif cw_kind == "cw":
                            p.op(SCA, lambda e, tb=tb, c=c, idx=idx, hh=hh: e.activation(
                                tmp_s[:, (tb % 2) * D:(tb % 2 + 1) * D],
                                py[tb % 2][:, 0:1024], AF.Copy,
                                scale=cwb[:, (c * 4 + tb) * E + idx:
                                          (c * 4 + tb) * E + idx + 1]),
                                reads=[r_py[tb % 2], r_cw[hh]], writes=[r_tmp[tb % 2]])
                        else:
                            p.op(SCA, lambda e, tb=tb, c=c, idx=idx, hh=hh: e.activation(
                                tmp_s[:, (tb % 2) * D:(tb % 2 + 1) * D],
                                py[tb % 2][:, 0:1024], AF.Copy,
                                scale=acwb[:, (c * 4 + tb) * 16 + idx:
                                           (c * 4 + tb) * 16 + idx + 1]),
                                reads=[r_py[tb % 2], r_cw[hh]], writes=[r_tmp[tb % 2]])
                        p.op(VEC, lambda e, c=c, tb=tb: e.tensor_tensor(
                            acc[:, (c * 4 + tb) * D:(c * 4 + tb + 1) * D],
                            acc[:, (c * 4 + tb) * D:(c * 4 + tb + 1) * D],
                            tmp_s[:, (tb % 2) * D:(tb % 2 + 1) * D], ALU.add),
                            reads=[r_tmp[tb % 2]], writes=[r_acc[c * 4 + tb]])
                nxt = u + 2
                if nxt < NU:
                    load_unit(nxt)
                elif hh == 0 and nxt - NU < 2:
                    load_unit(nxt - NU)

            for c in range(NCH):
                p.dma(DY,
                      xout[tok0 + c * 512: tok0 + (c + 1) * 512, :].rearrange(
                          "(tb q) d -> q tb d", q=128),
                      acc[:, :].rearrange("p (tb d) -> p tb d", d=D)[:, c * 4:(c + 1) * 4, :],
                      reads=r_acc[c * 4: c * 4 + 4])

    return p.fence()



def emit_ln4(p, L, reads_x):
    xa, lnb, sqd, stat = L["xa"], L["lnb"], L["sqd"], L["stat"]
    r_stat, r_sq, r_sqd, r_lnb = L["r_stat"], L["r_sq"], L["r_sqd"], L["r_lnb"]
    p.op(VEC, lambda e: e.tensor_reduce(
        stat[:, 0:4].rearrange("p (a o) -> p a o", o=1),
        xa[:, :].rearrange("p (tb d) -> p tb d", d=D), AX.X, ALU.add),
        reads=reads_x, writes=[r_stat])
    for tb in range(4):
        p.op(SCA, lambda e, tb=tb: e.activation(
            sqd[:, :], xa[:, tb * D:(tb + 1) * D],
            AF.Square, accum_out=stat[:, 8 + tb: 9 + tb]),
            reads=[reads_x[tb], r_sqd], writes=[r_sq[tb], r_sqd])
    p.op(VEC, lambda e: e.tensor_scalar(stat[:, 0:4], stat[:, 0:4],
                                        1.0 / D, None, ALU.mult),
         reads=[r_stat], writes=[r_stat])
    p.op(VEC, lambda e: e.tensor_scalar(stat[:, 12:16], stat[:, 8:12],
                                        1.0 / D, None, ALU.mult),
         reads=[r_sq[0], r_sq[1], r_sq[2], r_sq[3]], writes=[r_stat])
    p.op(VEC, lambda e: e.tensor_tensor(stat[:, 20:24], stat[:, 0:4],
                                        stat[:, 0:4], ALU.mult),
         reads=[r_stat], writes=[r_stat])
    p.op(VEC, lambda e: e.tensor_tensor(stat[:, 12:16], stat[:, 12:16],
                                        stat[:, 20:24], ALU.subtract),
         reads=[r_stat], writes=[r_stat])
    p.op(VEC, lambda e: e.tensor_scalar(stat[:, 12:16], stat[:, 12:16],
                                        1e-5, None, ALU.add),
         reads=[r_stat], writes=[r_stat])
    p.op(SCA, lambda e: e.activation(stat[:, 24:28], stat[:, 12:16], AF.Sqrt),
         reads=[r_stat], writes=[r_stat])
    p.op(VEC, lambda e: e.reciprocal(stat[:, 28:32], stat[:, 24:28]),
         reads=[r_stat], writes=[r_stat])
    for tb in range(4):
        p.op(VEC, lambda e, tb=tb: e.tensor_scalar(
            lnb[:, tb * D:(tb + 1) * D], xa[:, tb * D:(tb + 1) * D],
            stat[:, tb: tb + 1], stat[:, 28 + tb: 29 + tb],
            ALU.subtract, op1=ALU.mult),
            reads=[reads_x[tb], r_stat], writes=[r_lnb[tb]])



def emit_attn0(p, st, fence, lay, xin, xout, TC):
    nc = p.nc
    name = lay["name"]
    NCH = TC // 512
    ident = lay["ident"]

    p.barrier(fence)

    sb = st.enter_context
    wt_a = sb(nc.sbuf_tensor(f"{name}_wta", [128, 8192], BF16))
    wt_b = sb(nc.sbuf_tensor(f"{name}_wtb", [128, 8192], BF16))
    wnat = sb(nc.sbuf_tensor(f"{name}_wnat", [128, 8192], BF16))
    xa = sb(nc.sbuf_tensor(f"{name}_xa", [128, 4 * D], F32))
    lnb = sb(nc.sbuf_tensor(f"{name}_lnb", [128, 4 * D], BF16))
    sqd = sb(nc.sbuf_tensor(f"{name}_sqd", [128, D], F32))
    stat = sb(nc.sbuf_tensor(f"{name}_stat", [128, 48], F32))
    h_sb = sb(nc.sbuf_tensor(f"{name}_h", [128, 2 * 8 * 512], BF16))
    kf_tm = sb(nc.sbuf_tensor(f"{name}_kf", [128, 4 * D], BF16))
    v_tm = sb(nc.sbuf_tensor(f"{name}_vt", [128, 4 * D], BF16))
    kv_sb = sb(nc.sbuf_tensor(f"{name}_kv", [128, 512], BF16))
    ks_sb = sb(nc.sbuf_tensor(f"{name}_ks", [128, 8], F32))
    blkks = sb(nc.sbuf_tensor(f"{name}_bks", [128, 128], BF16))
    qf = sb(nc.sbuf_tensor(f"{name}_qf", [128, 8 * 512], BF16))
    rden = sb(nc.sbuf_tensor(f"{name}_rd", [16, 512], F32))
    rden_bf = sb(nc.sbuf_tensor(f"{name}_rdb", [16, 512], BF16))
    ao_sb = sb(nc.sbuf_tensor(f"{name}_ao", [128, 8 * 512], BF16))
    bo_b = sb(nc.sbuf_tensor(f"{name}_bob", [128, D], F32))
    blk1 = sb(nc.sbuf_tensor(f"{name}_blk", [128, 128], BF16))
    ekp = sb(nc.sbuf_tensor(f"{name}_ekp", [16, 1024], BF16))
    onescol = sb(nc.sbuf_tensor(f"{name}_oc", [128, 1], BF16))
    emn = sb(nc.sbuf_tensor(f"{name}_emn", [128, 2 * D], F32))
    eex = sb(nc.sbuf_tensor(f"{name}_eex", [128, 2 * D], F32))
    emx = sb(nc.sbuf_tensor(f"{name}_emx", [128, 2 * D], F32))
    otmp = sb(nc.sbuf_tensor(f"{name}_ot", [128, 2 * D], F32))

    h_scr = nc.dram_tensor(f"{name}_hscr", [D, TC], BF16)

    r_wta = Res(); r_wtb = Res(); r_wnat = Res()
    r_xa = [Res() for _ in range(4)]
    r_lnb = [Res() for _ in range(4)]
    r_stat = Res(); r_sq = [Res() for _ in range(4)]; r_sqd = Res()
    r_h = [[Res() for _ in range(4)] for _ in range(2)]
    r_hscr = [Res() for _ in range(NCH)]
    r_kf = [Res() for _ in range(4)]
    r_vt = [Res() for _ in range(4)]
    r_kv = Res(); r_ks = Res()
    r_kvsb = Res(); r_kssb = Res()
    r_qf = [Res() for _ in range(8)]
    r_blkks = Res()
    r_rden = Res(); r_rdenb = Res()
    r_ao = [Res() for _ in range(16)]
    r_elu = [Res(), Res()]
    r_otmp = [Res(), Res()]
    r_consts = Res()

    L = dict(xa=xa, lnb=lnb, sqd=sqd, stat=stat,
             r_stat=r_stat, r_sq=r_sq, r_sqd=r_sqd, r_lnb=r_lnb)

    p.dma(DG, bo_b[:, :], lay["bo_bcast"], writes=[r_consts])
    p.dma(DG, blk1[:, :], lay["blockones"], writes=[r_consts])
    p.dma(DG, ekp[0:16, :], lay["ekpat"], writes=[r_consts])
    p.dma(DG, onescol[:, :], lay["onescol"], writes=[r_consts])

    def load_transpose_w(w_ap, dst, r_dst, ptr, r_ptr, pf):
        p.dma(DG, wnat[:, :].rearrange("p (k i) -> p k i", k=8),
              w_ap.rearrange("(k q) i -> q k i", q=128), writes=[r_wnat])
        for o in range(8):
            for i in range(8):
                j = (o * 8 + i) % 2
                p.op(TEN, lambda e, o=o, i=i, j=j: e.transpose(
                    ptr[j][:, :], wnat[:, o * 1024 + i * 128: o * 1024 + (i + 1) * 128],
                    ident[:, :]),
                    reads=[r_wnat], writes=[r_ptr[j]], deps=pf)
                p.op(SCA, lambda e, o=o, i=i, j=j: e.copy(
                    dst[:, i * 1024 + o * 128: i * 1024 + (o + 1) * 128], ptr[j][:, :]),
                    reads=[r_ptr[j]], writes=[r_dst])

    with ExitStack() as p1:
        ptr = [p1.enter_context(nc.psum_tensor(f"{name}_pt{i}", [128, 128], BF16))
               for i in range(2)]
        pk = [p1.enter_context(nc.psum_tensor(f"{name}_pk{i}", [128, 1024], F32))
              for i in range(2)]
        pkv = p1.enter_context(nc.psum_tensor(f"{name}_pkv", [128, 512], F32))
        pks = p1.enter_context(nc.psum_tensor(f"{name}_pks", [128, 8], F32))
        r_ptr = [Res(), Res()]
        r_pk = [Res(), Res()]
        r_pkv = Res(); r_pks = Res()
        pf = p.fence()

        load_transpose_w(lay["wk"], wt_a, r_wta, ptr, r_ptr, pf)
        load_transpose_w(lay["wv"], wt_b, r_wtb, ptr, r_ptr, pf)

        kv_started = {0: False, 64: False}
        ks_started = [False]
        for c in range(NCH):
            par = c % 2
            hof = par * 4096
            p.dma(DY, xa[:, :].rearrange("p (tb d) -> p tb d", d=D),
                  xin[c * 512:(c + 1) * 512, :].rearrange("(tb q) d -> q tb d", q=128),
                  writes=r_xa)
            emit_ln4(p, L, r_xa)
            for tb in range(4):
                for k in range(8):
                    j = k % 2
                    p.op(TEN, lambda e, tb=tb, k=k, j=j: e.transpose(
                        ptr[j][:, :], lnb[:, tb * D + k * 128: tb * D + (k + 1) * 128],
                        ident[:, :]),
                        reads=[r_lnb[tb]], writes=[r_ptr[j]], deps=pf)
                    p.op(SCA, lambda e, tb=tb, k=k, j=j, hof=hof: e.copy(
                        h_sb[:, hof + k * 512 + tb * 128: hof + k * 512 + (tb + 1) * 128],
                        ptr[j][:, :]),
                        reads=[r_ptr[j]], writes=[r_h[par][tb]])
            p.dma(DY, h_scr[:, c * 512:(c + 1) * 512].rearrange("(k q) t -> q k t", q=128),
                  h_sb[:, hof:hof + 4096].rearrange("p (k t) -> p k t", k=8),
                  reads=[r_h[par][0], r_h[par][1], r_h[par][2], r_h[par][3]],
                  writes=[r_hscr[c]])
            kv_jobs = []
            for tb in range(4):
                for n in range(2):
                    def mkk(e, tb=tb, n=n, hof=hof):
                        mm = None
                        for k in range(8):
                            mm = e.matmul(
                                pk[0][:, n * 512:(n + 1) * 512],
                                h_sb[:, hof + k * 512 + tb * 128: hof + k * 512 + (tb + 1) * 128],
                                wt_a[:, k * 1024 + n * 512: k * 1024 + (n + 1) * 512],
                                start=(k == 0), stop=(k == 7))
                        return mm
                    p.op(TEN, mkk, reads=[r_h[par][tb], r_wta], writes=[r_pk[0]], deps=pf)
                pe = tb % 2
                p.op(VEC, lambda e, pe=pe: e.tensor_scalar(
                    emn[:, pe * D:(pe + 1) * D], pk[0][:, :], 0.0, None, ALU.min),
                    reads=[r_pk[0], r_elu[pe]], writes=[r_elu[pe]])
                p.op(SCA, lambda e, pe=pe: e.activation(
                    eex[:, pe * D:(pe + 1) * D], emn[:, pe * D:(pe + 1) * D], AF.Exp),
                    reads=[r_elu[pe]], writes=[r_elu[pe]])
                p.op(VEC, lambda e, pe=pe: e.tensor_scalar(
                    emx[:, pe * D:(pe + 1) * D], pk[0][:, :], 0.0, None, ALU.max),
                    reads=[r_pk[0], r_elu[pe]], writes=[r_elu[pe]])
                p.op(VEC, lambda e, tb=tb, pe=pe: e.tensor_tensor(
                    kf_tm[:, tb * D:(tb + 1) * D], eex[:, pe * D:(pe + 1) * D],
                    emx[:, pe * D:(pe + 1) * D], ALU.add),
                    reads=[r_elu[pe]], writes=[r_kf[tb]])
                for n in range(2):
                    def mkv(e, tb=tb, n=n, hof=hof):
                        mm = None
                        for k in range(8):
                            mm = e.matmul(
                                pk[1][:, n * 512:(n + 1) * 512],
                                h_sb[:, hof + k * 512 + tb * 128: hof + k * 512 + (tb + 1) * 128],
                                wt_b[:, k * 1024 + n * 512: k * 1024 + (n + 1) * 512],
                                start=(k == 0), stop=(k == 7))
                        return mm
                    p.op(TEN, mkv, reads=[r_h[par][tb], r_wtb], writes=[r_pk[1]], deps=pf)
                p.op(SCA, lambda e, tb=tb: e.copy(v_tm[:, tb * D:(tb + 1) * D], pk[1][:, :]),
                     reads=[r_pk[1]], writes=[r_vt[tb]])
            for tb in range(4):
                for h in range(16):
                    srow = 64 * (h % 2)
                    scol = (h // 2) * 64
                    st_flag = not kv_started[srow]
                    kv_started[srow] = True
                    last = (c == NCH - 1 and tb == 3 and h >= 14)
                    p.op(TEN, lambda e, tb=tb, h=h, srow=srow, scol=scol,
                         st_flag=st_flag, last=last: e.matmul(
                             pkv[srow:srow + 64, scol:scol + 64],
                             kf_tm[:, tb * D + h * 64: tb * D + (h + 1) * 64],
                             v_tm[:, tb * D + h * 64: tb * D + (h + 1) * 64],
                             start=st_flag, stop=last, skip_group_check=True),
                         reads=[r_kf[tb], r_vt[tb]], writes=[r_kv], deps=pf)
                for db in range(8):
                    st_flag = not ks_started[0]
                    ks_started[0] = True
                    last = (c == NCH - 1 and tb == 3 and db == 7)
                    p.op(TEN, lambda e, tb=tb, db=db, st_flag=st_flag, last=last: e.matmul(
                        pks[:, db:db + 1],
                        kf_tm[:, tb * D + db * 128: tb * D + (db + 1) * 128],
                        onescol[:, :],
                        start=st_flag, stop=last, skip_group_check=True),
                        reads=[r_kf[tb], r_consts], writes=[r_ks], deps=pf)
        p.op(VEC, lambda e: e.tensor_copy(kv_sb[:, :], pkv[:, :]),
             reads=[r_kv], writes=[r_kvsb])
        p.op(VEC, lambda e: e.tensor_copy(ks_sb[:, :], pks[:, :]),
             reads=[r_ks], writes=[r_kssb])
        for db in range(8):
            p.op(VEC, lambda e, db=db: e.tensor_scalar(
                blkks[:, db * 16:(db + 1) * 16], blk1[:, db * 16:(db + 1) * 16],
                ks_sb[:, db:db + 1], None, ALU.mult),
                reads=[r_kssb, r_consts, r_blkks], writes=[r_blkks])
        load_transpose_w(lay["wq"], wt_a, r_wta, ptr, r_ptr, pf)
        load_transpose_w(lay["wo"], wt_b, r_wtb, ptr, r_ptr, pf)

    with ExitStack() as p3:
        pool = [p3.enter_context(nc.psum_tensor(f"{name}_po{i}", [128, 1024], F32))
                for i in range(2)]
        pden = p3.enter_context(nc.psum_tensor(f"{name}_pd", [16, 512], F32))
        pdb = [p3.enter_context(nc.psum_tensor(f"{name}_pb{i}", [128, 512], F32))
               for i in range(2)]
        pao = p3.enter_context(nc.psum_tensor(f"{name}_pa", [128, 512], F32))
        r_pool = [Res() for _ in range(4)]
        r_pden = Res()
        r_pdbq = [Res(), Res(), Res()]
        pf3 = p.fence()
        ptensors = [pao, pdb[0], pdb[1]]

        def pool_region(i):
            return pool[i // 2][:, (i % 2) * 512:(i % 2) * 512 + 512]

        oproj_pending = [None]

        def flush_oproj():
            job = oproj_pending[0]
            if job is not None:
                oproj_pending[0] = None
                job()

        for c in range(NCH):
            par = c % 2
            hof = par * 4096
            p.dma(DY, h_sb[:, hof:hof + 4096].rearrange("p (k t) -> p k t", k=8),
                  h_scr[:, c * 512:(c + 1) * 512].rearrange("(k q) t -> q k t", q=128),
                  writes=[r_h[par][0], r_h[par][1], r_h[par][2], r_h[par][3]],
                  deps=[r_hscr[c].w])
            for db in range(8):
                reg = db % 4

                def mkq(e, db=db, reg=reg, hof=hof):
                    mm = None
                    for k in range(8):
                        mm = e.matmul(
                            pool_region(reg),
                            wt_a[:, k * 1024 + db * 128: k * 1024 + (db + 1) * 128],
                            h_sb[:, hof + k * 512: hof + (k + 1) * 512],
                            start=(k == 0), stop=(k == 7))
                    return mm
                p.op(TEN, mkq, reads=r_h[par] + [r_wta],
                     writes=[r_pool[reg]], deps=pf3)
                if db == 0:
                    flush_oproj()
                pe = db % 2
                p.op(VEC, lambda e, reg=reg, pe=pe: e.tensor_scalar(
                    emn[:, pe * 512:(pe + 1) * 512], pool_region(reg), 0.0, None, ALU.min),
                    reads=[r_pool[reg], r_elu[pe]], writes=[r_elu[pe]])
                p.op(SCA, lambda e, pe=pe: e.activation(
                    eex[:, pe * 512:(pe + 1) * 512], emn[:, pe * 512:(pe + 1) * 512], AF.Exp),
                    reads=[r_elu[pe]], writes=[r_elu[pe]])
                p.op(VEC, lambda e, reg=reg, pe=pe: e.tensor_scalar(
                    emx[:, pe * 512:(pe + 1) * 512], pool_region(reg), 0.0, None, ALU.max),
                    reads=[r_pool[reg], r_elu[pe]], writes=[r_elu[pe]])
                p.op(VEC, lambda e, db=db, pe=pe: e.tensor_tensor(
                    qf[:, db * 512:(db + 1) * 512], eex[:, pe * 512:(pe + 1) * 512],
                    emx[:, pe * 512:(pe + 1) * 512], ALU.add),
                    reads=[r_elu[pe]], writes=[r_qf[db]])

            def mkden(e):
                mm = None
                for db in range(8):
                    mm = e.matmul(pden[0:16, 0:512], blkks[:, db * 16:(db + 1) * 16],
                                  qf[:, db * 512:(db + 1) * 512],
                                  start=(db == 0), stop=(db == 7))
                return mm
            p.op(TEN, mkden, reads=[r_qf[i] for i in range(8)] + [r_blkks],
                 writes=[r_pden], deps=pf3)
            p.op(SCA, lambda e: e.copy(rden[:, :], pden[0:16, 0:512]),
                 reads=[r_pden], writes=[r_rden])
            p.op(VEC, lambda e: e.tensor_scalar(rden[:, :], rden[:, :], 1e-6, None, ALU.add),
                 reads=[r_rden], writes=[r_rden])
            p.op(VEC, lambda e: e.reciprocal(rden[:, :], rden[:, :]),
                 reads=[r_rden], writes=[r_rden])
            p.op(VEC, lambda e: e.tensor_copy(rden_bf[:, :], rden[:, :]),
                 reads=[r_rden], writes=[r_rdenb])
            for db in range(8):
                j = db % 2
                p.op(TEN, lambda e, db=db, j=j: e.matmul(
                    pdb[j][:, :], ekp[0:16, db * 128:(db + 1) * 128],
                    rden_bf[0:16, :], start=True, stop=True),
                    reads=[r_rdenb], writes=[r_pdbq[1 + j]], deps=pf3)
                p.op(VEC, lambda e, db=db, j=j: e.tensor_tensor(
                    qf[:, db * 512:(db + 1) * 512], qf[:, db * 512:(db + 1) * 512],
                    pdb[j][:, :], ALU.mult),
                    reads=[r_qf[db], r_pdbq[1 + j]], writes=[r_qf[db]])
            for h in range(16):
                db = h // 2
                srow = 64 * (h % 2)
                rot = h % 3

                def mkqkv(e, h=h, db=db, srow=srow, rot=rot):
                    t = ptensors[rot]
                    return e.matmul(t[srow:srow + 64, 0:512],
                                    kv_sb[srow:srow + 64,
                                          (h // 2) * 64:(h // 2 + 1) * 64],
                                    qf[srow:srow + 64, db * 512:(db + 1) * 512],
                                    start=True, stop=True)
                p.op(TEN, mkqkv, reads=[r_kvsb, r_qf[db]], writes=[r_pdbq[rot]], deps=pf3)
                p.op(SCA, lambda e, h=h, db=db, srow=srow, rot=rot: e.copy(
                    ao_sb[srow:srow + 64, db * 512:(db + 1) * 512],
                    ptensors[rot][srow:srow + 64, 0:512]),
                    reads=[r_pdbq[rot]], writes=[r_ao[h]])
            def oproj_job(c=c):
                p.dma(DY, xa[:, :].rearrange("p (tb d) -> p tb d", d=D),
                      xin[c * 512:(c + 1) * 512, :].rearrange("(tb q) d -> q tb d", q=128),
                      writes=r_xa)
                for tb in range(4):
                    pt = pool[tb % 2]
                    regs = [r_pool[(tb % 2) * 2], r_pool[(tb % 2) * 2 + 1]]

                    def mko(e, tb=tb, pt=pt):
                        mm = None
                        for n in range(2):
                            for k in range(8):
                                mm = e.matmul(
                                    pt[:, n * 512:(n + 1) * 512],
                                    ao_sb[:, k * 512 + tb * 128: k * 512 + (tb + 1) * 128],
                                    wt_b[:, k * 1024 + n * 512: k * 1024 + (n + 1) * 512],
                                    start=(k == 0), stop=(k == 7))
                        return mm
                    p.op(TEN, mko, reads=[r_ao[i] for i in range(16)] + [r_wtb],
                         writes=regs, deps=pf3)
                    p.op(VEC, lambda e, tb=tb, pt=pt: e.tensor_tensor(
                        otmp[:, (tb % 2) * D:(tb % 2 + 1) * D], pt[:, 0:1024],
                        bo_b[:, :], ALU.add),
                        reads=regs + [r_consts], writes=[r_otmp[tb % 2]])
                    p.op(VEC, lambda e, tb=tb: e.tensor_tensor(
                        xa[:, tb * D:(tb + 1) * D], xa[:, tb * D:(tb + 1) * D],
                        otmp[:, (tb % 2) * D:(tb % 2 + 1) * D], ALU.add),
                        reads=[r_otmp[tb % 2]], writes=[r_xa[tb]])
                p.dma(DY, xout[c * 512:(c + 1) * 512, :].rearrange("(tb q) d -> q tb d", q=128),
                      xa[:, :].rearrange("p (tb d) -> p tb d", d=D),
                      reads=r_xa)
            oproj_pending[0] = oproj_job
        flush_oproj()

    return p.fence()



def emit_attn1(p, st, fence, lay, xin, xout, TC):
    nc = p.nc
    name = lay["name"]
    NCH = TC // 512
    NTB_TOT = TC // 128
    ident = lay["ident"]

    p.barrier(fence)

    sb = st.enter_context
    wq_n = sb(nc.sbuf_tensor(f"{name}_wq", [128, 8192], BF16))
    xa = sb(nc.sbuf_tensor(f"{name}_xa", [128, 4 * D], F32))
    stat = sb(nc.sbuf_tensor(f"{name}_stat", [128, 48], F32))
    aff = sb(nc.sbuf_tensor(f"{name}_aff", [128, 12 * 8], F32))
    affc = sb(nc.sbuf_tensor(f"{name}_affc", [128, 6 * 8], F32))
    v_tm = sb(nc.sbuf_tensor(f"{name}_vt", [128, NTB_TOT * D], BF16))
    tq_row = sb(nc.sbuf_tensor(f"{name}_tqr", [1, 512], F32))
    tkb_row = sb(nc.sbuf_tensor(f"{name}_tkr", [1, 512], F32))
    tkcol = sb(nc.sbuf_tensor(f"{name}_tkc", [128, TC // 128], F32))
    bo_b = sb(nc.sbuf_tensor(f"{name}_bob", [128, D], F32))
    onescol = sb(nc.sbuf_tensor(f"{name}_oc", [128, 1], BF16))
    onescolf = sb(nc.sbuf_tensor(f"{name}_ocf", [128, 1], F32))
    ones1r = sb(nc.sbuf_tensor(f"{name}_o1r", [1, 128], F32))
    otmp = sb(nc.sbuf_tensor(f"{name}_ot", [128, 2 * D], F32))

    tq_scr = nc.dram_tensor(f"{name}_tqscr", [1, TC], F32)

    r_wq = Res(); r_wk = Res(); r_wv = Res()
    r_tqs = [Res() for _ in range(NCH)]
    r_xa = [Res() for _ in range(4)]
    r_lnb = [Res() for _ in range(4)]
    r_stat = Res(); r_sq = [Res() for _ in range(4)]; r_sqd = Res()
    r_h = [Res() for _ in range(4)]
    r_aff = Res(); r_affc = Res()
    r_qbf = [Res(), Res()]
    r_vtp = [Res(), Res()]
    r_vtm = [Res() for _ in range(NTB_TOT)]
    r_tq = Res(); r_tkb = Res(); r_tkc = Res()
    r_tqb = Res()
    r_at = [Res(), Res()]
    r_datc = Res()
    r_os = [Res() for _ in range(8)]
    r_otmp = [Res(), Res()]
    r_consts = Res()

    p.dma(DG, bo_b[:, :], lay["bo_bcast"], writes=[r_consts])
    p.dma(DG, onescol[:, :], lay["onescol"], writes=[r_consts])
    p.dma(DG, onescolf[:, :], lay["onescolf"], writes=[r_consts])
    p.dma(DG, ones1r[0:1, :], lay["ones1r"], writes=[r_consts])
    p.dma(DG, wq_n[:, :].rearrange("p (k i) -> p k i", k=8),
          lay["Wq"].rearrange("(k q) i -> q k i", q=128), writes=[r_wq])
    for i in range(12):
        p.dma(DG, aff[:, i * 8:(i + 1) * 8], lay["aff_cols"][i], writes=[r_aff])

    for j in range(3):
        g0 = (4 * j + 0) * 8
        b0 = (4 * j + 1) * 8
        m0 = (4 * j + 2) * 8
        v0 = (4 * j + 3) * 8
        a0 = (2 * j) * 8
        c0 = (2 * j + 1) * 8
        p.op(VEC, lambda e, v0=v0, c0=c0: e.tensor_scalar(
            affc[:, c0:c0 + 8], aff[:, v0:v0 + 8], 1e-5, None, ALU.add),
            reads=[r_aff, r_affc], writes=[r_affc])
        p.op(SCA, lambda e, c0=c0: e.activation(affc[:, c0:c0 + 8],
                                                affc[:, c0:c0 + 8], AF.Sqrt),
             reads=[r_affc], writes=[r_affc])
        p.op(VEC, lambda e, c0=c0: e.reciprocal(affc[:, c0:c0 + 8], affc[:, c0:c0 + 8]),
             reads=[r_affc], writes=[r_affc])
        p.op(VEC, lambda e, g0=g0, c0=c0, a0=a0: e.tensor_tensor(
            affc[:, a0:a0 + 8], aff[:, g0:g0 + 8], affc[:, c0:c0 + 8], ALU.mult),
            reads=[r_aff, r_affc], writes=[r_affc])
        p.op(VEC, lambda e, m0=m0, a0=a0, c0=c0: e.tensor_tensor(
            affc[:, c0:c0 + 8], aff[:, m0:m0 + 8], affc[:, a0:a0 + 8], ALU.mult),
            reads=[r_aff, r_affc], writes=[r_affc])
        p.op(VEC, lambda e, a0=a0: e.tensor_scalar(
            affc[:, a0:a0 + 8], affc[:, a0:a0 + 8], 4.0, None, ALU.mult),
            reads=[r_affc], writes=[r_affc])
        p.op(VEC, lambda e, c0=c0: e.tensor_scalar(
            affc[:, c0:c0 + 8], affc[:, c0:c0 + 8], -4.0, -4.0, ALU.mult, op1=ALU.add),
            reads=[r_affc], writes=[r_affc])
        p.op(VEC, lambda e, b0=b0: e.tensor_scalar(
            stat[:, 40:48], aff[:, b0:b0 + 8], 4.0, None, ALU.mult),
            reads=[r_aff, r_affc], writes=[r_affc])
        p.op(VEC, lambda e, c0=c0: e.tensor_tensor(
            affc[:, c0:c0 + 8], affc[:, c0:c0 + 8], stat[:, 40:48], ALU.add),
            reads=[r_affc], writes=[r_affc])

    with ExitStack() as p1:
        wk_n = p1.enter_context(nc.sbuf_tensor(f"{name}_wk", [128, 8192], BF16))
        wv_n = p1.enter_context(nc.sbuf_tensor(f"{name}_wv", [128, 8192], BF16))
        qbf = p1.enter_context(nc.sbuf_tensor(f"{name}_qb", [128, 2 * 512], F32))
        vtp = p1.enter_context(nc.sbuf_tensor(f"{name}_vtp", [128, 2 * 512], BF16))
        lnb = p1.enter_context(nc.sbuf_tensor(f"{name}_lnb", [128, 4 * D], BF16))
        sqd = p1.enter_context(nc.sbuf_tensor(f"{name}_sqd", [128, D], F32))
        h_sb = p1.enter_context(nc.sbuf_tensor(f"{name}_h", [128, 8 * 512], BF16))
        ptr = [p1.enter_context(nc.psum_tensor(f"{name}_pt{i}", [128, 128], BF16))
               for i in range(2)]
        pool3 = [p1.enter_context(nc.psum_tensor(f"{name}_pp{i}", [128, 512], F32))
                 for i in range(4)]
        pqs = p1.enter_context(nc.psum_tensor(f"{name}_pqs", [1, 512], F32))
        pks2 = p1.enter_context(nc.psum_tensor(f"{name}_pks", [1, 512], F32))
        r_ptr = [Res(), Res()]
        r_pool3 = [Res() for _ in range(4)]
        r_pqs = Res(); r_pks2 = Res()
        pf = p.fence()

        p.dma(DG, wk_n[:, :].rearrange("p (k i) -> p k i", k=8),
              lay["Wk"].rearrange("(k q) i -> q k i", q=128), writes=[r_wk])
        p.dma(DG, wv_n[:, :].rearrange("p (k i) -> p k i", k=8),
              lay["Wv"].rearrange("(k q) i -> q k i", q=128), writes=[r_wv])

        L = dict(xa=xa, lnb=lnb, sqd=sqd, stat=stat,
                 r_stat=r_stat, r_sq=r_sq, r_sqd=r_sqd, r_lnb=r_lnb)

        for c in range(NCH):
            p.dma(DY, xa[:, :].rearrange("p (tb d) -> p tb d", d=D),
                  xin[c * 512:(c + 1) * 512, :].rearrange("(tb q) d -> q tb d", q=128),
                  writes=r_xa)
            emit_ln4(p, L, r_xa)
            for tb in range(4):
                for k in range(8):
                    j = k % 2
                    p.op(TEN, lambda e, tb=tb, k=k, j=j: e.transpose(
                        ptr[j][:, :], lnb[:, tb * D + k * 128: tb * D + (k + 1) * 128],
                        ident[:, :]),
                        reads=[r_lnb[tb]], writes=[r_ptr[j]], deps=pf)
                    p.op(SCA, lambda e, tb=tb, k=k, j=j: e.copy(
                        h_sb[:, k * 512 + tb * 128: k * 512 + (tb + 1) * 128], ptr[j][:, :]),
                        reads=[r_ptr[j]], writes=[r_h[tb]])
            for (wres, wbuf, prow, r_prow, acst) in (
                    (r_wq, wq_n, pqs, r_pqs, 0), (r_wk, wk_n, pks2, r_pks2, 2)):
                def emit_sum(db, prow=prow, r_prow=r_prow):
                    jb = db % 2
                    p.op(TEN, lambda e, db=db, jb=jb, prow=prow: e.matmul(
                        prow[0:1, 0:512], onescolf[:, :], qbf[:, jb * 512:(jb + 1) * 512],
                        start=(db == 0), stop=(db == 7), skip_group_check=True),
                        reads=[r_qbf[jb], r_consts], writes=[r_prow], deps=pf)
                for db in range(8):
                    reg = db % 4

                    def mkp(e, db=db, reg=reg, wbuf=wbuf):
                        mm = None
                        for k in range(8):
                            mm = e.matmul(
                                pool3[reg][:, :],
                                wbuf[:, k * 1024 + db * 128: k * 1024 + (db + 1) * 128],
                                h_sb[:, k * 512:(k + 1) * 512],
                                start=(k == 0), stop=(k == 7))
                        return mm
                    p.op(TEN, mkp, reads=[r_h[0], r_h[1], r_h[2], r_h[3], wres],
                         writes=[r_pool3[reg]], deps=pf)
                    jb = db % 2
                    p.op(SCA, lambda e, db=db, reg=reg, jb=jb, acst=acst: e.activation(
                        qbf[:, jb * 512:(jb + 1) * 512], pool3[reg][:, :], AF.Sigmoid,
                        bias=affc[:, (acst + 1) * 8 + db: (acst + 1) * 8 + db + 1],
                        scale=affc[:, acst * 8 + db: acst * 8 + db + 1]),
                        reads=[r_pool3[reg], r_affc], writes=[r_qbf[jb]])
                    if db >= 1:
                        emit_sum(db - 1)
                emit_sum(7)
            p.op(VEC, lambda e, c=c: e.tensor_scalar(
                tq_row[0:1, 0:512], pqs[0:1, 0:512],
                -1.0 / D, 1.0, ALU.mult, op1=ALU.add),
                reads=[r_pqs, r_tq], writes=[r_tq])
            p.dma(DY, tq_scr[0:1, c * 512:(c + 1) * 512], tq_row[0:1, 0:512],
                  reads=[r_tq], writes=[r_tqs[c]])
            p.op(VEC, lambda e: e.tensor_scalar(
                tkb_row[0:1, 0:512], pks2[0:1, 0:512],
                -1.0 / D, 1.0, ALU.mult, op1=ALU.add),
                reads=[r_pks2], writes=[r_tkb])
            for j4 in range(4):
                reg4 = j4 % 4
                p.op(TEN, lambda e, j4=j4, reg4=reg4: e.matmul(
                    pool3[reg4][:, 0:1], tkb_row[0:1, j4 * 128:(j4 + 1) * 128],
                    ones1r[0:1, 0:1], start=True, stop=True),
                    reads=[r_tkb, r_consts], writes=[r_pool3[reg4]], deps=pf)
                p.op(SCA, lambda e, c=c, j4=j4, reg4=reg4: e.copy(
                    tkcol[:, c * 4 + j4: c * 4 + j4 + 1], pool3[reg4][:, 0:1]),
                    reads=[r_pool3[reg4]], writes=[r_tkc])
            def emit_vtr(db, c=c):
                jb = db % 2
                for j4 in range(4):
                    jj = j4 % 2
                    p.op(TEN, lambda e, db=db, jb=jb, j4=j4, jj=jj: e.transpose(
                        ptr[jj][:, :], vtp[:, jb * 512 + j4 * 128: jb * 512 + (j4 + 1) * 128],
                        ident[:, :]),
                        reads=[r_vtp[jb]], writes=[r_ptr[jj]], deps=pf)
                    p.op(SCA, lambda e, c=c, db=db, j4=j4, jj=jj: e.copy(
                        v_tm[:, (c * 4 + j4) * D + db * 128: (c * 4 + j4) * D + (db + 1) * 128],
                        ptr[jj][:, :]),
                        reads=[r_ptr[jj]], writes=[r_vtm[c * 4 + j4]])
            for db in range(8):
                reg = db % 4

                def mkvf(e, db=db, reg=reg):
                    mm = None
                    for k in range(8):
                        mm = e.matmul(
                            pool3[reg][:, :],
                            wv_n[:, k * 1024 + db * 128: k * 1024 + (db + 1) * 128],
                            h_sb[:, k * 512:(k + 1) * 512],
                            start=(k == 0), stop=(k == 7))
                    return mm
                p.op(TEN, mkvf, reads=[r_h[0], r_h[1], r_h[2], r_h[3], r_wv],
                     writes=[r_pool3[reg]], deps=pf)
                jb = db % 2
                p.op(SCA, lambda e, db=db, reg=reg, jb=jb: e.activation(
                    vtp[:, jb * 512:(jb + 1) * 512], pool3[reg][:, :], AF.Sigmoid,
                    bias=affc[:, 5 * 8 + db: 5 * 8 + db + 1],
                    scale=affc[:, 4 * 8 + db: 4 * 8 + db + 1]),
                    reads=[r_pool3[reg], r_affc], writes=[r_vtp[jb]])
                if db >= 1:
                    emit_vtr(db - 1)
            emit_vtr(7)

    p.dma(DG, wq_n[:, :].rearrange("p (k i) -> p k i", k=8),
          lay["Wo"].rearrange("(k q) i -> q k i", q=128), writes=[r_wq])

    with ExitStack() as p2:
        at_sb = p2.enter_context(nc.sbuf_tensor(f"{name}_at", [128, 2 * 32 * 512], BF16))
        o_sb = p2.enter_context(nc.sbuf_tensor(f"{name}_os", [128, 8 * 512], BF16))
        tqb = p2.enter_context(nc.sbuf_tensor(f"{name}_tqb", [128, 512], F32))
        dat = p2.enter_context(nc.sbuf_tensor(f"{name}_dat", [128, 2 * 512], F32))
        dex = p2.enter_context(nc.sbuf_tensor(f"{name}_dex", [128, 2 * 512], F32))
        dm2 = p2.enter_context(nc.sbuf_tensor(f"{name}_dm2", [128, 2 * 512], F32))
        zt = p2.enter_context(nc.sbuf_tensor(f"{name}_zt", [128, 512], F32))
        pO = [p2.enter_context(nc.psum_tensor(f"{name}_pO{i}", [128, 512], F32))
              for i in range(8)]
        r_pO = [Res() for _ in range(8)]
        r_datp = [Res(), Res()]
        pf2 = p.fence()

        def at_build_pre(c):
            p.dma(DY, tq_row[0:1, 0:512], tq_scr[0:1, c * 512:(c + 1) * 512],
                  reads=[], writes=[r_tq], deps=[r_tqs[c].w])
            p.op(TEN, lambda e: e.matmul(
                pO[0][:, :], ones1r[0:1, :], tq_row[0:1, 0:512],
                start=True, stop=True),
                reads=[r_consts, r_tq], writes=[r_pO[0]], deps=pf2)
            p.op(SCA, lambda e: e.copy(tqb[:, :], pO[0][:, :]),
                 reads=[r_pO[0]], writes=[r_tqb])

        atb_pending = [None]

        def at_tail(c):
            bb = c % 2
            s2t = atb_pending[0]
            if s2t is None:
                return
            atb_pending[0] = None
            pe = s2t % 2
            o5 = pe * 512
            p.op(VEC, lambda e, o5=o5, pe=pe: e.tensor_tensor(
                dm2[:, o5:o5 + 512], dm2[:, o5:o5 + 512], dex[:, o5:o5 + 512],
                ALU.mult),
                reads=[r_datp[pe]], writes=[r_datp[pe]])
            p.op(VEC, lambda e, bb=bb, s2t=s2t, o5=o5, pe=pe: e.tensor_scalar(
                at_sb[:, bb * 32 * 512 + s2t * 512: bb * 32 * 512 + (s2t + 1) * 512],
                dm2[:, o5:o5 + 512], -0.9, 0.9, ALU.mult, op1=ALU.add),
                reads=[r_datp[pe]], writes=[r_at[bb]])

        def at_build(c, tile_range=None):
            if tile_range is None:
                tile_range = range(NTB_TOT)
            for s2t in tile_range:
                pe = s2t % 2
                o5 = pe * 512
                p.op(VEC, lambda e, s2t=s2t, o5=o5, pe=pe: e.tensor_scalar(
                    dat[:, o5:o5 + 512], tqb[:, :], tkcol[:, s2t:s2t + 1], None,
                    ALU.subtract),
                    reads=[r_tqb, r_tkc, r_datp[pe]], writes=[r_datp[pe]])
                at_tail(c)
                p.op(SCA, lambda e, o5=o5, pe=pe: e.activation(
                    dm2[:, o5:o5 + 512], dat[:, o5:o5 + 512], AF.Sign),
                    reads=[r_datp[pe]], writes=[r_datp[pe]])
                p.op(SCA, lambda e, o5=o5, pe=pe: e.activation(
                    dex[:, o5:o5 + 512], dat[:, o5:o5 + 512], AF.Abs),
                    reads=[r_datp[pe]], writes=[r_datp[pe]])
                p.op(SCA, lambda e, o5=o5, pe=pe: e.activation(
                    dex[:, o5:o5 + 512], dex[:, o5:o5 + 512], AF.Exp, scale=-2.0),
                    reads=[r_datp[pe]], writes=[r_datp[pe]])
                atb_pending[0] = s2t
                if s2t == NTB_TOT - 1:
                    at_tail(c)

        at_build_pre(0)
        at_build(0)
        for c in range(NCH):
            bb = c % 2
            if c + 1 < NCH:
                at_build_pre(c + 1)
            for eb in range(8):
                reg = eb

                def mkO(e, eb=eb, reg=reg, bb=bb):
                    mm = None
                    for s2t in range(NTB_TOT):
                        mm = e.matmul(
                            pO[reg][:, :],
                            v_tm[:, s2t * D + eb * 128: s2t * D + (eb + 1) * 128],
                            at_sb[:, bb * 32 * 512 + s2t * 512: bb * 32 * 512 + (s2t + 1) * 512],
                            start=(s2t == 0), stop=(s2t == NTB_TOT - 1))
                    return mm
                p.op(TEN, mkO, reads=[r_at[bb]] + r_vtm, writes=[r_pO[reg]], deps=pf2)
                p.op(SCA, lambda e, eb=eb, reg=reg: e.copy(
                    o_sb[:, eb * 512:(eb + 1) * 512], pO[reg][:, :]),
                    reads=[r_pO[reg]], writes=[r_os[eb]])
                if c + 1 < NCH:
                    n_per = NTB_TOT // 8
                    at_build(c + 1, range(eb * n_per, (eb + 1) * n_per))
            p.dma(DY, xa[:, :].rearrange("p (tb d) -> p tb d", d=D),
                  xin[c * 512:(c + 1) * 512, :].rearrange("(tb q) d -> q tb d", q=128),
                  writes=r_xa)
            for tb in range(4):
                b0 = 4 + (tb % 2) * 2
                regs = [r_pO[b0], r_pO[b0 + 1]]

                def mko(e, tb=tb, b0=b0):
                    mm = None
                    for n in range(2):
                        for k in range(8):
                            mm = e.matmul(
                                pO[b0 + n][:, :],
                                o_sb[:, k * 512 + tb * 128: k * 512 + (tb + 1) * 128],
                                wq_n[:, k * 1024 + n * 512: k * 1024 + (n + 1) * 512],
                                start=(k == 0), stop=(k == 7))
                    return mm
                p.op(TEN, mko, reads=[r_os[i] for i in range(8)] + [r_wq],
                     writes=regs, deps=pf2)
                for n in range(2):
                    p.op(VEC, lambda e, tb=tb, b0=b0, n=n: e.tensor_tensor(
                        otmp[:, (tb % 2) * D + n * 512: (tb % 2) * D + (n + 1) * 512],
                        pO[b0 + n][:, :], bo_b[:, n * 512:(n + 1) * 512], ALU.add),
                        reads=[regs[n], r_consts], writes=[r_otmp[tb % 2]])
                p.op(VEC, lambda e, tb=tb: e.tensor_tensor(
                    xa[:, tb * D:(tb + 1) * D], xa[:, tb * D:(tb + 1) * D],
                    otmp[:, (tb % 2) * D:(tb % 2 + 1) * D], ALU.add),
                    reads=[r_otmp[tb % 2]], writes=[r_xa[tb]])
            p.dma(DY, xout[c * 512:(c + 1) * 512, :].rearrange("(tb q) d -> q tb d", q=128),
                  xa[:, :].rearrange("p (tb d) -> p tb d", d=D),
                  reads=r_xa)

    return p.fence()



def build_program(TC=4096, subs=(1, 2, 3, 4, 5, 6)):
    nc = bass.Bass("TRN2", target_bir_lowering=False, debug=False)

    inp = {}

    def din(nm, shape, dt=F32):
        inp[nm] = nc.dram_tensor(nm, list(shape), dt, kind="ExternalInput")
        return inp[nm]

    x_in = din("x", (TC, D))
    din("L0_ffn_gate_weight", (32, D))
    din("L0_ffn_w_gate", (32, D, 512))
    din("L0_ffn_w_up", (32, D, 512))
    din("L0_ffn_w_down", (32, 512, D))
    din("L0_ffn_shared_w_gate", (D, 512))
    din("L0_ffn_shared_w_up", (D, 512))
    din("L0_ffn_shared_w_down", (512, D))
    din("L0_ffn_anchor_w_gate", (16, D, 256))
    din("L0_ffn_anchor_w_up", (16, D, 256))
    din("L0_ffn_anchor_w_down", (16, 256, D))
    din("gbb0", (128, 32))
    din("L1_ffn_gate_kernel", (D, 16))
    din("L1_ffn_w_gate", (16, D, 512))
    din("L1_ffn_w_up", (16, D, 512))
    din("L1_ffn_w_down", (16, 512, D))
    din("L1_ffn_sw_gate", (D, D))
    din("L1_ffn_sw_up", (D, D))
    din("L1_ffn_sw_down", (D, D))
    din("gbb1", (128, 16))
    din("ident", (128, 128), BF16)
    din("L0_attn_wq", (D, D))
    din("L0_attn_wk", (D, D))
    din("L0_attn_wv", (D, D))
    din("L0_attn_wo", (D, D))
    din("bob0", (128, D))
    din("L1_attn_Wq", (D, D))
    din("L1_attn_Wk", (D, D))
    din("L1_attn_Wv", (D, D))
    din("L1_attn_Wo", (D, D))
    din("bob1", (128, D))
    din("blockones", (128, 128), BF16)
    din("ekpat", (16, 1024), BF16)
    din("onescol", (128, 1), BF16)
    din("onescolf", (128, 1))
    din("ones1r", (1, 128))
    AFF_NAMES = ["gq", "bq", "mq", "vq", "gk", "bk", "mk", "vk",
                 "gv", "bv", "mv", "vv"]
    for nm in AFF_NAMES:
        din(f"aff_{nm}", (128, 8))

    x_scr = nc.dram_tensor("x_scr", [TC, D], F32)
    y_out = nc.dram_tensor("y", [TC, D], F32, kind="ExternalOutput")

    with ExitStack() as st:
        p = Prog(nc, st)
        blk = st.enter_context(nc.Block())
        ident = st.enter_context(nc.sbuf_tensor("ident_sb", [128, 128], BF16))
        r_ident = Res()
        p.dma(DG, ident[:, :], inp["ident"][:, :], writes=[r_ident])
        fence = p.fence()

        def l0_units():
            us = [(inp["L0_ffn_shared_w_gate"][:, :], inp["L0_ffn_shared_w_up"][:, :],
                   inp["L0_ffn_shared_w_down"][:, :], 512, "const", 0)]
            for e in range(32):
                us.append((inp["L0_ffn_w_gate"][e, :, :], inp["L0_ffn_w_up"][e, :, :],
                           inp["L0_ffn_w_down"][e, :, :], 512, "cw", e))
            for l in range(16):
                us.append((inp["L0_ffn_anchor_w_gate"][l, :, :],
                           inp["L0_ffn_anchor_w_up"][l, :, :],
                           inp["L0_ffn_anchor_w_down"][l, :, :], 256, "acw", l))
            return us

        def l1_units():
            us = []
            for h2 in range(2):
                us.append((inp["L1_ffn_sw_gate"][:, h2 * 512:(h2 + 1) * 512],
                           inp["L1_ffn_sw_up"][:, h2 * 512:(h2 + 1) * 512],
                           inp["L1_ffn_sw_down"][h2 * 512:(h2 + 1) * 512, :],
                           512, "const", 0))
            for e in range(16):
                us.append((inp["L1_ffn_w_gate"][e, :, :], inp["L1_ffn_w_up"][e, :, :],
                           inp["L1_ffn_w_down"][e, :, :], 512, "cw", e))
            return us

        lay_l0 = dict(E=32, n_group=4, top_k=6, route_scale=0.2,
                      gate="l0", gwT_src=("transpose", inp["L0_ffn_gate_weight"][:, :]),
                      gb_bcast=inp["gbb0"][:, :], units=l0_units(), ident=ident)
        lay_l1 = dict(E=16, n_group=4, top_k=4, route_scale=2.5,
                      gate="l1", gwT_src=("direct", inp["L1_ffn_gate_kernel"][:, :]),
                      gb_bcast=inp["gbb1"][:, :], units=l1_units(), ident=ident)

        lay_a0 = dict(name="a0", wq=inp["L0_attn_wq"][:, :], wk=inp["L0_attn_wk"][:, :],
                      wv=inp["L0_attn_wv"][:, :], wo=inp["L0_attn_wo"][:, :],
                      bo_bcast=inp["bob0"][:, :], blockones=inp["blockones"][:, :],
                      ekpat=inp["ekpat"][:, :], onescol=inp["onescol"][:, :],
                      ident=ident)
        lay_a1 = dict(name="a1", Wq=inp["L1_attn_Wq"][:, :], Wk=inp["L1_attn_Wk"][:, :],
                      Wv=inp["L1_attn_Wv"][:, :], Wo=inp["L1_attn_Wo"][:, :],
                      bo_bcast=inp["bob1"][:, :], onescol=inp["onescol"][:, :],
                      onescolf=inp["onescolf"][:, :], ones1r=inp["ones1r"][:, :],
                      aff_cols=[inp[f"aff_{nm}"][:, :] for nm in AFF_NAMES],
                      ident=ident)

        seq = []
        if 1 in subs:
            seq.append(("ffn", dict(lay_l0, name="f0a")))
        if 2 in subs:
            seq.append(("attn0", lay_a0))
        if 3 in subs:
            seq.append(("ffn", dict(lay_l0, name="f0b")))
        if 4 in subs:
            seq.append(("ffn", dict(lay_l1, name="f1a")))
        if 5 in subs:
            seq.append(("attn1", lay_a1))
        if 6 in subs:
            seq.append(("ffn", dict(lay_l1, name="f1b")))

        cur = x_in
        for i, (kindl, lay) in enumerate(seq):
            last = (i == len(seq) - 1)
            xout = y_out if last else x_scr
            scope_name = f"s{i+1}_{kindl}_{lay['name'] if lay else ''}"
            with ExitStack() as sub_st:
                with nc.named_scope(scope_name):
                    if kindl == "ffn":
                        fence = emit_ffn(p, sub_st, fence, lay, cur, xout, TC)
                    elif kindl == "attn0":
                        fence = emit_attn0(p, sub_st, fence, lay, cur, xout, TC)
                    else:
                        fence = emit_attn1(p, sub_st, fence, lay, cur, xout, TC)
            cur = xout

        for q in (DY, DG):
            for i in range(NSLOT):
                nm = f"{q}{i}"
                if p.cnt[nm] > 0:
                    nc.sync.wait_ge(p.sem[nm], p.cnt[nm])

    return nc


WEIGHT_KEYS = ["L0_ffn_gate_weight", "L0_ffn_w_gate", "L0_ffn_w_up", "L0_ffn_w_down",
               "L0_ffn_shared_w_gate", "L0_ffn_shared_w_up", "L0_ffn_shared_w_down",
               "L0_ffn_anchor_w_gate", "L0_ffn_anchor_w_up", "L0_ffn_anchor_w_down",
               "L1_ffn_gate_kernel", "L1_ffn_w_gate", "L1_ffn_w_up", "L1_ffn_w_down",
               "L1_ffn_sw_gate", "L1_ffn_sw_up", "L1_ffn_sw_down"]


def make_in_maps(inputs, n_cores=8):
    x = np.ascontiguousarray(np.asarray(inputs["L0_attn_x"], dtype=np.float32))
    base = {}
    for k in WEIGHT_KEYS:
        base[k] = np.ascontiguousarray(np.asarray(inputs[k], dtype=np.float32))
    base["gbb0"] = np.ascontiguousarray(np.broadcast_to(
        np.asarray(inputs["L0_ffn_gate_bias"], dtype=np.float32)[None, :], (128, 32)))
    base["gbb1"] = np.ascontiguousarray(np.broadcast_to(
        np.asarray(inputs["L1_ffn_gate_bias"], dtype=np.float32)[None, :], (128, 16)))
    base["ident"] = np.eye(128, dtype=ml_dtypes.bfloat16)
    for k in ["L0_attn_wq", "L0_attn_wk", "L0_attn_wv", "L0_attn_wo",
              "L1_attn_Wq", "L1_attn_Wk", "L1_attn_Wv", "L1_attn_Wo"]:
        base[k] = np.ascontiguousarray(np.asarray(inputs[k], dtype=np.float32))
    base["bob0"] = np.ascontiguousarray(np.broadcast_to(
        np.asarray(inputs["L0_attn_bo"], dtype=np.float32)[None, :], (128, D)))
    base["bob1"] = np.ascontiguousarray(np.broadcast_to(
        np.asarray(inputs["L1_attn_bo"], dtype=np.float32)[None, :], (128, D)))
    blk = np.zeros((128, 128), dtype=ml_dtypes.bfloat16)
    for db in range(8):
        for r in range(128):
            blk[r, db * 16 + 2 * db + r // 64] = 1
    base["blockones"] = blk
    ekp = np.zeros((16, 1024), dtype=ml_dtypes.bfloat16)
    for db in range(8):
        for r in range(128):
            ekp[2 * db + r // 64, db * 128 + r] = 1
    base["ekpat"] = ekp
    base["onescol"] = np.ones((128, 1), dtype=ml_dtypes.bfloat16)
    base["onescolf"] = np.ones((128, 1), dtype=np.float32)
    base["ones1r"] = np.ones((1, 128), dtype=np.float32)
    for nm in ["gq", "bq", "mq", "vq", "gk", "bk", "mk", "vk", "gv", "bv", "mv", "vv"]:
        v = np.asarray(inputs[f"L1_attn_{nm}"], dtype=np.float32)
        base[f"aff_{nm}"] = np.ascontiguousarray(v.reshape(8, 128).T)
    maps = []
    for c in range(n_cores):
        m = dict(base)
        m["x"] = np.ascontiguousarray(x[c])
        maps.append(m)
    return maps


def kernel(**inputs):
    nc = build_program(TC=4096, subs=(1, 2, 3, 4, 5, 6))
    maps = make_in_maps(inputs, 8)
    res = run_bass_kernel_spmd(nc, maps, core_ids=list(range(8)))
    out = np.stack([res.results[i]["y"] for i in range(8)], axis=0)
    return out.astype(np.float32)
